# version 28
# baseline (speedup 1.0000x reference)
import time
import numpy as np
import ml_dtypes
import concourse.bass as bass
import concourse.mybir as mybir
from concourse.bass_utils import run_bass_kernel_spmd

F32 = mybir.dt.float32
BF16 = mybir.dt.bfloat16
U32 = mybir.dt.uint32
I32 = mybir.dt.int32
ALU = mybir.AluOpType
ACT = mybir.ActivationFunctionType
AX = mybir.AxisListType.X

N_CORES = 8
NSEQ = 2
T = 2048
C = 1024
H = 16
NTB = T // 128
NQC = T // 512
CLIP = 128.0 - 1e-5
LN1_EPS = 1e-5
LN2_EPS = 1e-5
FFN_LN_EPS = 1e-3
PAIR_GAMMA = False

ENGS = ("sync", "scalar", "tensor", "vector", "gpsimd")
ESEM = {"sync": ("sy", 16), "scalar": ("sc", 1), "tensor": ("pe", 1),
        "vector": ("dv", 1), "gpsimd": ("gp", 1)}


class Op:
    __slots__ = ("eng", "fn", "waits", "sem", "needed", "val")

    def __init__(self, eng, fn, waits, sem):
        self.eng = eng
        self.fn = fn
        self.waits = [(w, r) for (w, r) in waits if w is not None]
        self.sem = sem
        self.needed = False
        self.val = None


class Sched:

    def __init__(self):
        self.ops = {e: [] for e in ENGS}
        self.order = []

    def emit(self, eng, fn, waits=(), dma=None):
        sem = (dma, 16) if dma else ESEM[eng]
        op = Op(eng, fn, waits, sem)
        if dma:
            op.needed = True
        self.ops[eng].append(op)
        self.order.append(op)
        return op

    @staticmethod
    def _edge_kept(op, w, raw):
        if w.eng != op.eng or w.sem[1] == 16:
            return True
        return raw and op.eng != "tensor"

    def finalize(self):
        for op in self.order:
            for w, raw in op.waits:
                if self._edge_kept(op, w, raw):
                    w.needed = True
        cnt = {}
        for op in self.order:
            if op.needed:
                name, inc = op.sem
                cnt[name] = cnt.get(name, 0) + inc
                op.val = (name, cnt[name])

    def replay(self, eng_name, eng, sems):
        waited = {}
        for op in self.ops[eng_name]:
            tgt = {}
            for w, raw in op.waits:
                if not self._edge_kept(op, w, raw):
                    continue
                if w.val is None:
                    continue
                n, v = w.val
                if v > tgt.get(n, -1):
                    tgt[n] = v
            for n, v in sorted(tgt.items()):
                if v > waited.get(n, 0):
                    eng.wait_ge(sems[n], v)
                    waited[n] = v
            inst = op.fn(eng)
            if op.needed and op.val is not None and inst is not None:
                inst.then_inc(sems[op.val[0]], op.sem[1])


class Buf:
    __slots__ = ("w", "r")

    def __init__(self):
        self.w = None
        self.r = []

    def rd_waits(self):
        return [self.w] if self.w is not None else []

    def wr_waits(self):
        return list(self.r) + ([self.w] if self.w is not None else [])

    def rd(self, op):
        self.r.append(op)

    def wrote(self, op):
        self.w = op
        self.r = []


def build_nc():
    nc = bass.Bass("TRN2", target_bir_lowering=False, detect_race_conditions=False)

    x_h = nc.dram_tensor("x", [NSEQ, T, C], F32, kind="ExternalInput").ap()
    wa_h = nc.dram_tensor("w_attn", [3 * C, C], F32, kind="ExternalInput").ap()
    wp_h = nc.dram_tensor("w_proj", [C, C], F32, kind="ExternalInput").ap()
    wf_h = nc.dram_tensor("w_ffn", [C, C], F32, kind="ExternalInput").ap()
    g_h = nc.dram_tensor("lng", [1, C], F32, kind="ExternalInput").ap()
    b_h = nc.dram_tensor("lnb", [1, C], F32, kind="ExternalInput").ap()
    nz_h = nc.dram_tensor("nonce", [1, 8], U32, kind="ExternalInput").ap()
    out_h = nc.dram_tensor("out", [NSEQ, T, C], F32, kind="ExternalOutput").ap()

    x1_d = nc.dram_tensor("x1_spill", [NSEQ * T, C], F32, kind="Internal").ap()
    xn_d = nc.dram_tensor("xn_spill", [NSEQ * T, C], F32, kind="Internal").ap()
    wqk_d = nc.dram_tensor("wq_qk", [128, 16 * C], BF16, kind="Internal").ap()
    wqv_d = nc.dram_tensor("wq_v", [128, 8 * C], BF16, kind="Internal").ap()
    wqp_d = nc.dram_tensor("wq_p", [128, 8 * C], BF16, kind="Internal").ap()
    wqf_d = nc.dram_tensor("wq_f", [128, 8 * C], BF16, kind="Internal").ap()
    sh = nc.dram_tensor("sh_pair", [2, 16], F32, kind="Internal", addr_space="Shared").ap()
    sh_u = sh.bitcast(U32)

    ident_bf_h = nc.inline_tensor(np.eye(128, dtype=ml_dtypes.bfloat16), "c_ident_bf").ap()
    ident_f_h = nc.inline_tensor(np.eye(128, dtype=np.float32), "c_ident_f").ap()
    mk = np.tril(np.ones((128, 128), np.float32)).T.astype(ml_dtypes.bfloat16)
    mask_h = nc.inline_tensor(mk, "causal_mask").ap()

    from contextlib import ExitStack
    ctx = ExitStack()
    sb = lambda n, s, d: ctx.enter_context(nc.sbuf_tensor(n, s, d))
    psumt = lambda n, s, d=F32: ctx.enter_context(nc.psum_tensor(n, s, d))

    budget = [0]

    def sbt(name, shape, dtype):
        per = int(np.prod(shape[1:])) * (2 if dtype == BF16 else 4)
        budget[0] += per
        return sb(name, shape, dtype)

    ident_bf = sbt("ident_bf", [128, 128], BF16)
    ident_f = sbt("ident_f", [128, 128], F32)
    mask_bf = sbt("mask_bf", [128, 128], BF16)
    ones_col = sbt("ones_col", [1, 128], F32)
    ones_128 = sbt("ones_128", [128, 1], F32)
    ones_rl = sbt("ones_rl", [128, 64], BF16)
    Gb = sbt("G_bcast", [128, C], F32)
    Bb = sbt("B_bcast", [128, C], F32)
    scl = sbt("scl", [128, 16], F32)
    sc_row = sbt("sc_row", [1, 16], F32)
    rec_row = sbt("rec_row", [1, 8], F32)
    peer_row = sbt("peer_row", [1, 8], F32)
    poll_row = sbt("poll_row", [1, 8], U32)
    nz_sb = sbt("nz_sb", [1, 8], U32)
    gacc = sbt("gacc", [128, 1], F32)
    gtmp = sbt("gtmp", [128, 4], F32)
    stat = sbt("stat", [128, 16], F32)
    wsum = sbt("wsum", [128, 40], F32)
    epsc = sbt("epsc", [128, 2], F32)
    Wqk_r = [sbt(f"Wqkr{i}", [128, C], BF16) for i in range(2)]
    Warena = sbt("Warena", [128, 8 * C], BF16)
    Kt = sbt("Kt", [128, 8 * T], BF16)
    Vt = sbt("Vt", [128, NTB * 1040], BF16)
    Yf = sbt("Yf", [128, 8 * T], BF16)
    Qr = [sbt(f"Qr{i}", [128, 8 * 512], BF16) for i in range(2)]
    Xbf = [sbt(f"Xbf{i}", [128, 8 * 512], BF16) for i in range(2)]
    Pt = [sbt(f"Pt{i}", [128, 512], BF16) for i in range(6)]
    xtmp = [sbt(f"xtmp{i}", [128, C], BF16) for i in range(2)]
    F = [sbt(f"F{i}", [128, C], F32) for i in range(8)]
    rl_f = sbt("rl_f", [65, 512], F32)
    rl_bf = [sbt(f"rl_bf{i}", [65, 512], BF16) for i in range(2)]

    assert budget[0] <= 211000, f"SBUF budget: {budget[0]} B/partition"

    xq_tm = [Xbf[0][:, 0:C], Xbf[0][:, C:2 * C]]
    xq_fm = [Xbf[1][:, 0:C], Xbf[1][:, C:2 * C]]
    wq_st = [Yf[:, i * C:(i + 1) * C] for i in range(3)]
    sg_st = [Yf[:, (3 + i) * C:(4 + i) * C] for i in range(3)]
    tr_st = [Yf[:, (6 + i) * C:(7 + i) * C] for i in range(2)]

    PB = [psumt(f"PB{i}", [128, 512]) for i in range(2)]
    YB = [psumt(f"YB{i}", [65, 512]) for i in range(2)]
    RLB = psumt("RLB", [64, 512])
    PSB = psumt("PSB", [128, 1024], BF16)
    SPB = [psumt(f"SPB{i}", [128, 512]) for i in range(2)]
    PR = [PB[0], PB[1], SPB[0], SPB[1]]

    sems = {n: nc.alloc_semaphore(name=n) for n in
            ("sc", "pe", "dv", "gp", "cn0", "cn1", "cn2", "cn3", "f0", "f1",
             "f2", "f3", "f4", "f5", "f6", "f7", "wk0", "wk1", "ar", "sk",
             "sp", "sf", "sv", "g1", "g2", "g3")}
    psem = nc.alloc_semaphore(name="poll")

    S = Sched()

    b_consts = Buf()
    b_Gb, b_Bb, b_scl, b_scrow = Buf(), Buf(), Buf(), Buf()
    b_wsum = Buf()
    b_Wqk_r = [Buf(), Buf()]
    b_Warena = Buf()
    b_Kt, b_Vt, b_Yf = Buf(), Buf(), Buf()
    b_Qr = [Buf(), Buf()]
    b_Xbf = [Buf(), Buf()]
    b_Pt = [Buf() for _ in range(6)]
    b_xtmp = [Buf(), Buf()]
    b_F = [Buf() for _ in range(8)]
    b_PB = [Buf(), Buf()]
    b_YB = [Buf(), Buf()]
    b_RLB, b_PSB = Buf(), Buf()
    b_SPB = [Buf(), Buf()]
    b_PR = [b_PB[0], b_PB[1], b_SPB[0], b_SPB[1]]
    b_rlf = Buf()
    b_rlbf = [Buf(), Buf()]
    b_gacc, b_gtmp, b_stat = Buf(), Buf(), Buf()
    b_x1d = Buf()
    b_xnd = Buf()
    b_wqk, b_wqv, b_wqp, b_wqf = Buf(), Buf(), Buf(), Buf()
    b_nz, b_peer = Buf(), Buf()

    def dma(eng, out_ap, in_ap, sem, rbuf=None, wbuf=None, extra=()):
        waits = [(w, True) for w in extra]
        if rbuf is not None:
            waits += [(w, True) for w in rbuf.rd_waits()]
        if wbuf is not None:
            waits += [(w, False) for w in wbuf.wr_waits()]
        op = S.emit(eng, lambda e, o=out_ap, i=in_ap: e.dma_start(out=o, in_=i),
                    waits, dma=sem)
        if rbuf is not None:
            rbuf.rd(op)
        if wbuf is not None:
            wbuf.wrote(op)
        return op

    def comp(eng, fn, reads=(), writes=(), extra=()):
        waits = [(w, True) for w in extra]
        for b in reads:
            waits += [(w, True) for w in b.rd_waits()]
        for b in writes:
            waits += [(w, False) for w in b.wr_waits()]
        op = S.emit(eng, fn, waits)
        for b in reads:
            b.rd(op)
        for b in writes:
            b.wrote(op)
        return op

    def vec(fn, reads=(), writes=(), extra=()):
        return comp("vector", fn, reads, writes, extra)

    def group(eng, fns, reads=(), writes=(), extra=()):
        waits = [(w, True) for w in extra]
        for b in reads:
            waits += [(w, True) for w in b.rd_waits()]
        for b in writes:
            waits += [(w, False) for w in b.wr_waits()]
        ops = []
        for j, fn in enumerate(fns):
            ops.append(S.emit(eng, fn, waits if j == 0 else [(ops[-1], False)]))
        for b in reads:
            b.rd(ops[-1])
        for b in writes:
            b.wrote(ops[-1])
        return ops[-1]

    dma("sync", ident_bf[:], ident_bf_h[:], "cn0", wbuf=b_consts)
    dma("sync", ident_f[:], ident_f_h[:], "cn1", wbuf=b_consts)
    dma("sync", mask_bf[:], mask_h[:], "cn2", wbuf=b_consts)
    dma("sync", nz_sb[:], nz_h[:], "cn3", wbuf=b_nz)
    vec(lambda e: e.memset(ones_col[:], 1.0), writes=(b_consts,))
    vec(lambda e: e.memset(ones_128[:], 1.0), writes=(b_consts,))
    vec(lambda e: e.memset(ones_rl[:], 1.0), writes=(b_consts,))
    vec(lambda e: e.memset(gacc[:], 0.0), writes=(b_gacc,))
    vec(lambda e: e.memset(sc_row[:], 0.0), writes=(b_scrow,))
    vec(lambda e: e.memset(epsc[:, 0:1], LN1_EPS), writes=(b_consts,))
    vec(lambda e: e.memset(epsc[:, 1:2], FFN_LN_EPS), writes=(b_consts,))
    vt3 = Vt[:].rearrange("p (tb h f) -> p tb h f", tb=NTB, h=H, f=65)
    vec(lambda e: e.memset(vt3[:, :, :, 64:65], 1.0), writes=(b_Vt,))

    dma("sync", F[0][0:1, :], g_h[:], "f0", wbuf=b_F[0])
    dma("sync", F[1][0:1, :], b_h[:], "f1", wbuf=b_F[1])
    for src, bbuf, dst, bdst in ((F[0], b_F[0], Gb, b_Gb), (F[1], b_F[1], Bb, b_Bb)):
        for hf in range(2):
            comp("tensor", lambda e, s=src, h=hf: e.matmul(
                SPB[0][:, 0:512], ones_col[:], s[0:1, h * 512:(h + 1) * 512],
                start=True, stop=True),
                reads=(bbuf, b_consts), writes=(b_SPB[0],))
            vec(lambda e, h=hf, d=dst: e.tensor_copy(
                out=d[:, h * 512:(h + 1) * 512], in_=SPB[0][:, 0:512]),
                reads=(b_SPB[0],), writes=(bdst,))

    def wsrc(ob):
        if ob < 24:
            return wa_h[ob * 128:(ob + 1) * 128, :]
        if ob < 32:
            return wp_h[(ob - 24) * 128:(ob - 23) * 128, :]
        return wf_h[(ob - 32) * 128:(ob - 31) * 128, :]

    for ob in range(40):
        slot = ob % 3
        dma("sync" if ob % 2 == 0 else "scalar", F[slot][:], wsrc(ob),
            f"f{slot}", wbuf=b_F[slot])
        vec(lambda e, s=slot, o=ob: e.tensor_reduce(
            out=wsum[:, o:o + 1], in_=F[s][:], axis=AX, op=ALU.add,
            apply_absolute_value=True),
            reads=(b_F[slot],), writes=(b_wsum,))
    comp("tensor", lambda e: e.matmul(SPB[0][0:1, 0:40], ones_128[:], wsum[:, 0:40],
                                      start=True, stop=True),
         reads=(b_wsum, b_consts), writes=(b_SPB[0],))

    def sr(c):
        return sc_row[0:1, c:c + 1]

    vec(lambda e: e.tensor_reduce(out=sr(2), in_=SPB[0][0:1, 0:24], axis=AX, op=ALU.add),
        reads=(b_SPB[0],), writes=(b_scrow,))
    vec(lambda e: e.tensor_reduce(out=sr(4), in_=SPB[0][0:1, 24:32], axis=AX, op=ALU.add),
        reads=(b_SPB[0],), writes=(b_scrow,))
    vec(lambda e: e.tensor_reduce(out=sr(6), in_=SPB[0][0:1, 32:40], axis=AX, op=ALU.add),
        reads=(b_SPB[0],), writes=(b_scrow,))
    n_attn, n_pf = float(24 * 128 * 1024), float(8 * 128 * 1024)
    vec(lambda e: e.tensor_scalar(out=sr(2), in0=sr(2), scalar1=1.0 / n_attn,
                                  scalar2=1e-8, op0=ALU.mult, op1=ALU.max),
        reads=(b_scrow,), writes=(b_scrow,))
    vec(lambda e: e.tensor_scalar(out=sr(4), in0=sr(4), scalar1=1.0 / n_pf,
                                  scalar2=1e-8, op0=ALU.mult, op1=ALU.max),
        reads=(b_scrow,), writes=(b_scrow,))
    vec(lambda e: e.tensor_scalar(out=sr(6), in0=sr(6), scalar1=1.0 / n_pf,
                                  scalar2=None, op0=ALU.mult),
        reads=(b_scrow,), writes=(b_scrow,))
    vec(lambda e: e.reciprocal(out=sr(0), in_=sr(2)), reads=(b_scrow,), writes=(b_scrow,))
    vec(lambda e: e.reciprocal(out=sr(3), in_=sr(4)), reads=(b_scrow,), writes=(b_scrow,))
    vec(lambda e: e.tensor_scalar(out=sr(15), in0=sr(6), scalar1=1e-5, scalar2=None,
                                  op0=ALU.add),
        reads=(b_scrow,), writes=(b_scrow,))
    vec(lambda e: e.reciprocal(out=sr(5), in_=sr(15)), reads=(b_scrow,), writes=(b_scrow,))
    vec(lambda e: e.tensor_scalar(out=sr(1), in0=sr(2), scalar1=sr(2), scalar2=0.125,
                                  op0=ALU.mult, op1=ALU.mult),
        reads=(b_scrow,), writes=(b_scrow,))
    comp("scalar", lambda e: e.activation(out=sr(10), in_=sr(4), func=ACT.Ln),
         reads=(b_scrow,), writes=(b_scrow,))

    def bcast_scrow():
        comp("tensor", lambda e: e.matmul(SPB[0][:, 0:16], ones_col[:], sc_row[:],
                                          start=True, stop=True),
             reads=(b_scrow, b_consts), writes=(b_SPB[0],))
        vec(lambda e: e.tensor_copy(out=scl[:], in_=SPB[0][:, 0:16]),
            reads=(b_SPB[0],), writes=(b_scl,))

    bcast_scrow()

    psb3 = PSB[:].rearrange("p (ch t) -> p ch t", ch=8, t=128)

    def transpose8(src_ap, rbufs):
        fns = [(lambda e, j=i, sp=src_ap: e.matmul(
            PSB[:, j * 128:(j + 1) * 128], sp[:, j * 128:(j + 1) * 128], ident_bf[:],
            is_transpose=True, start=(j == 0), stop=(j == 7))) for i in range(8)]
        return group("tensor", fns, reads=tuple(rbufs) + (b_consts,), writes=(b_PSB,))

    vtail = lambda i: Vt[:, 12480 + i * 1040:12480 + i * 1040 + C]

    def prep_ob(ob, deferred=False, queue=None):
        slot, qs = ob % 2, ob % 3
        dma(queue or ("sync" if ob % 2 == 0 else "scalar"), F[slot][:], wsrc(ob),
            f"f{slot}", wbuf=b_F[slot])
        sc_col = scl[:, 0:1] if ob < 24 else (scl[:, 3:4] if ob < 32 else scl[:, 5:6])
        a, q = 2 + qs, 5 + qs
        if deferred:
            wq_t, sg_t, tr_t, b_st = vtail(0), vtail(1), vtail(2), b_Vt
        else:
            wq_t, sg_t, tr_t, b_st = wq_st[qs], sg_st[qs], tr_st[ob % 2], b_Yf
        comp("scalar", lambda e, s=slot, aa=a, c=sc_col: e.activation(
            out=F[aa][:], in_=F[s][:], func=ACT.Abs, scale=c),
            reads=(b_F[slot], b_scl), writes=(b_F[a],))
        comp("scalar", lambda e, s=slot, o=sg_t: e.sign(out=o, in_=F[s][:]),
             reads=(b_F[slot],), writes=(b_st,))
        vec(lambda e, aa=a, qq=q: e.tensor_scalar(
            out=F[qq][:], in0=F[aa][:], scalar1=0.5, scalar2=None, op0=ALU.is_ge),
            reads=(b_F[a],), writes=(b_F[q],))
        if ob < 32:
            vec(lambda e, aa=a, qq=q: e.scalar_tensor_tensor(
                out=F[qq][:], in0=F[aa][:], scalar=1.5, in1=F[qq][:],
                op0=ALU.is_ge, op1=ALU.add),
                reads=(b_F[a], b_F[q]), writes=(b_F[q],))
        vec(lambda e, qq=q, wo=wq_t, so=sg_t: e.tensor_tensor(
            out=wo, in0=F[qq][:], in1=so, op=ALU.mult),
            reads=(b_F[q], b_st), writes=(b_st,))

        if ob >= 32:
            ch = ob - 32
            dma("gpsimd", wqf_d[:, ch * C:(ch + 1) * C], wq_t, "sf",
                rbuf=b_st, wbuf=b_wqf)
            return
        transpose8(wq_t, (b_st,))
        tr = tr_t
        if ob < 16:
            vec(lambda e, o=tr: e.tensor_copy(out=o, in_=psb3[:].rearrange("p a b -> p (a b)")),
                reads=(b_PSB,), writes=(b_st,))
            dma("gpsimd", wqk_d[:, ob * C:(ob + 1) * C], tr, "sk", rbuf=b_st, wbuf=b_wqk)
        elif ob < 24:
            wa3 = Warena[:].rearrange("p (ch v) -> p ch v", ch=8, v=C)
            vec(lambda e, o=ob - 16: e.tensor_copy(
                out=wa3[:, :, o * 128:(o + 1) * 128], in_=psb3[:]),
                reads=(b_PSB,), writes=(b_Warena,))
        else:
            vec(lambda e, o=tr: e.tensor_copy(out=o, in_=psb3[:].rearrange("p a b -> p (a b)")),
                reads=(b_PSB,), writes=(b_st,))
            o = ob - 24
            wq3d = wqp_d[:].rearrange("p (ch v) -> p ch v", ch=8, v=C)
            st3 = tr.rearrange("p (ch t) -> p ch t", ch=8, t=128)
            dma("gpsimd", wq3d[:, :, o * 128:(o + 1) * 128], st3[:], "sp", rbuf=b_st, wbuf=b_wqp)

    for ob in range(16, 24):
        prep_ob(ob)
    dma("gpsimd", wqv_d[:], Warena[:], "sv", rbuf=b_Warena, wbuf=b_wqv)

    par_holder = {}

    def gp_pre(e):
        par = e.alloc_register("par")
        for arm in e.switch_core_id(n=128):
            e.reg_mov(par, int(arm) % 2)
        par_holder["par"] = par
        return None

    S.emit("gpsimd", gp_pre, ())

    seq_tb = lambda s, tb: s * NTB + tb

    def qkv_units(s, tc):
        xs = tc % 2
        xb3 = Xbf[xs][:].rearrange("p (ch t) -> p ch t", ch=8, t=512)
        units = []

        def unit_tbl(tbl):
                tb = tc * 4 + tbl
                fslot, xt = tb % 3, tb % 2
                dma("sync", F[fslot][:], x_h[s, tb * 128:(tb + 1) * 128, :], f"f{fslot}", wbuf=b_F[fslot])
                comp("scalar", lambda e, fs=fslot, x=xt: e.activation(
                    out=xtmp[x][:], in_=F[fs][:], func=ACT.Copy),
                    reads=(b_F[fslot],), writes=(b_xtmp[xt],))
                transpose8(xtmp[xt], (b_xtmp[xt],))
                vec(lambda e, t=tbl: e.tensor_copy(
                    out=xb3[:, :, t * 128:(t + 1) * 128], in_=psb3[:]),
                    reads=(b_PSB,), writes=(b_Xbf[xs],))
                for hf in range(2):
                    pb = (tb * 2 + hf) % 4
                    fns = [(lambda e, c=cc, h=hf, t=tbl, p=pb: e.matmul(
                        PR[p][:], xb3[:, c, t * 128:(t + 1) * 128],
                        Warena[:, c * C + h * 512:c * C + h * 512 + 512],
                        start=(c == 0), stop=(c == 7))) for cc in range(8)]
                    group("tensor", fns, reads=(b_Xbf[xs], b_Warena), writes=(b_PR[pb],))
                    vo = vt3[:, tb, hf * 8:(hf + 1) * 8, 0:64]
                    vec(lambda e, o=vo, p=pb: e.tensor_scalar(
                        out=o, in0=PR[p][:], scalar1=scl[:, 2:3], scalar2=None, op0=ALU.mult),
                        reads=(b_PR[pb], b_scl), writes=(b_Vt,))

        def unit_ob(ob):
                wr = ob % 2
                dma("sync", Wqk_r[wr][:], wqk_d[:, ob * C:(ob + 1) * C], f"wk{wr}",
                    rbuf=b_wqk, wbuf=b_Wqk_r[wr])
                pb = ob % 4
                fns = [(lambda e, c=cc, w=wr, p=pb: e.matmul(
                    PR[p][:], Wqk_r[w][:, c * 128:(c + 1) * 128],
                    Xbf[xs][:, c * 512:(c + 1) * 512],
                    start=(c == 0), stop=(c == 7))) for cc in range(8)]
                group("tensor", fns, reads=(b_Wqk_r[wr], b_Xbf[xs]), writes=(b_PR[pb],))
                if ob < 8:
                    vec(lambda e, o=ob, p=pb: e.tensor_scalar(
                        out=Qr[tc % 2][:, o * 512:(o + 1) * 512], in0=PR[p][:],
                        scalar1=scl[:, 1:2], scalar2=None, op0=ALU.mult),
                        reads=(b_PR[pb], b_scl), writes=(b_Qr[tc % 2],))
                else:
                    vec(lambda e, o=ob - 8, p=pb: e.tensor_copy(
                        out=Kt[:, o * T + tc * 512: o * T + (tc + 1) * 512], in_=PR[p][:]),
                        reads=(b_PR[pb],), writes=(b_Kt,))

        for tbl in range(4):
            units.append(lambda t=tbl: unit_tbl(t))
        for ob in range(16):
            units.append(lambda o=ob: unit_ob(o))
        return units

    def qkv_tc(s, tc):
        for u in qkv_units(s, tc):
            u()

    attn_gate = [None]

    def attention_qc(s, qc, fills=()):
        qs = qc % 2
        nkb = 4 * qc + 4
        sring, bring = PR, b_PR

        def head_stream(h, exp_gate_in):
            yb = (qc * H + h) % 2
            chunk, prow = h // 2, (h % 2) * 64
            exp_ops = {}
            prod_ops = {}

            def geom(kb):
                return (kb - 4 * qc) * 128 if kb >= 4 * qc else 0

            def mm_S(kb, gate):
                qlo, r = geom(kb), kb % 4
                waits = ([(gate, True)] if gate is not None else [])
                waits += [(w, False) for w in bring[r].wr_waits()]
                op = S.emit("tensor", lambda e, k=kb, q=qlo, rr=r: e.matmul(
                    sring[rr][:, q:512],
                    Kt[prow:prow + 64, chunk * T + k * 128: chunk * T + (k + 1) * 128],
                    Qr[qs][prow:prow + 64, chunk * 512 + q:(chunk + 1) * 512],
                    start=True, stop=True),
                    waits)
                b_Kt.rd(op)
                b_Qr[qs].rd(op)
                bring[r].wrote(op)
                return op

            def do_exp(kb, gate):
                qlo, r, pr = geom(kb), kb % 4, kb % 6
                op = S.emit("scalar", lambda e, q=qlo, rr=r, p=pr: e.activation(
                    out=Pt[p][:, q:512], in_=sring[rr][:, q:512], func=ACT.Exp),
                    [(gate, True)] if gate is not None else [])
                op.needed = True
                bring[r].rd(op)
                b_Pt[pr].wrote(op)
                exp_ops[kb] = op
                if kb >= 4 * qc:
                    mop = S.emit("vector", lambda e, q=qlo, p=pr: e.tensor_tensor(
                        out=Pt[p][:, q:q + 128], in0=Pt[p][:, q:q + 128],
                        in1=mask_bf[:], op=ALU.mult), [(op, True)])
                    mop.needed = True
                    b_consts.rd(mop)
                    b_Pt[pr].wrote(mop)
                    prod_ops[kb] = mop
                else:
                    prod_ops[kb] = op
                return op

            def mm_AV(kb, gate):
                qlo, pr = geom(kb), kb % 6
                fn = lambda e, k=kb, q=qlo, p=pr: e.matmul(
                    YB[yb][:, q:512], Vt[:, k * 1040 + h * 65: k * 1040 + (h + 1) * 65],
                    Pt[p][:, q:512], start=(k == 0), stop=(k == nkb - 1))
                waits = [(gate, True)]
                if kb == 0:
                    waits += [(w, False) for w in b_YB[yb].wr_waits()]
                else:
                    waits += [(b_YB[yb].w, False)]
                op = S.emit("tensor", fn, waits)
                b_Vt.rd(op)
                b_Pt[pr].rd(op)
                b_YB[yb].wrote(op) if kb == nkb - 1 else b_YB[yb].__setattr__("w", op)

            for j in range(nkb // 2):
                a, b = 2 * j, 2 * j + 1
                gate = exp_ops.get(a - 3, exp_gate_in if a < 4 else None)
                mm_S(a, gate)
                sgb = mm_S(b, None)
                do_exp(a, sgb)
                do_exp(b, None)
                if a >= 4:
                    mm_AV(a - 4, prod_ops[a - 4])
                    mm_AV(b - 4, prod_ops[a - 3])
            for kk in range(max(0, nkb - 4), nkb):
                mm_AV(kk, prod_ops[kk])
            return yb, exp_ops[nkb - 1]

        def tail_a(h, yb):
            rb = h % 2
            comp("scalar", lambda e, y=yb: e.activation(
                out=rl_f[64:65, :], in_=YB[y][64:65, :], func=ACT.Ln),
                reads=(b_YB[yb],), writes=(b_rlf,))
            comp("scalar", lambda e, r=rb: e.activation(
                out=rl_bf[r][64:65, :], in_=rl_f[64:65, :], func=ACT.Exp,
                scale=-1.0, bias=scl[64:65, 10:11]),
                reads=(b_rlf, b_scl), writes=(b_rlbf[rb],))
            vec(lambda e, y=yb: e.tensor_copy(out=rl_f[0:64, :], in_=YB[y][0:64, :]),
                reads=(b_YB[yb],), writes=(b_rlf,))

        def tail_b(h):
            chunk, prow, rb = h // 2, (h % 2) * 64, h % 2
            comp("tensor", lambda e, r=rb: e.matmul(
                RLB[:], ones_rl[64:65, 0:64], rl_bf[r][64:65, :], start=True, stop=True),
                reads=(b_rlbf[rb], b_consts), writes=(b_RLB,))
            ysl = Yf[prow:prow + 64, chunk * T + qc * 512: chunk * T + (qc + 1) * 512]
            vec(lambda e, o=ysl: e.tensor_tensor(
                out=o, in0=rl_f[0:64, :], in1=RLB[0:64, :], op=ALU.mult),
                reads=(b_rlf, b_RLB), writes=(b_Yf,))

        ybs = {}
        gate = attn_gate[0]
        done = 0
        for h in range(H):
            ybs[h], gate = head_stream(h, gate)
            if h >= 2:
                tail_b(h - 2)
            if h >= 1:
                tail_a(h - 1, ybs[h - 1])
            while done * H < (h + 1) * len(fills):
                fills[done]()
                done += 1
        tail_b(H - 2)
        tail_a(H - 1, ybs[H - 1])
        tail_b(H - 1)
        while done < len(fills):
            fills[done]()
            done += 1
        attn_gate[0] = gate

    def ln_stats_apply(u_slot, eps, out_slot, junk):
        vec(lambda e: e.tensor_add(out=stat[:, 10:11], in0=stat[:, 0:1], in1=stat[:, 1:2]),
            reads=(b_stat,), writes=(b_stat,))
        vec(lambda e: e.tensor_scalar(out=stat[:, 10:11], in0=stat[:, 10:11],
                                      scalar1=1.0 / C, scalar2=None, op0=ALU.mult),
            reads=(b_stat,), writes=(b_stat,))
        comp("scalar", lambda e, u=u_slot, j=junk: e.activation(
            out=F[j][:], in_=F[u][:], func=ACT.Square, accum_out=stat[:, 11:12]),
            reads=(b_F[u_slot],), writes=(b_F[junk], b_stat))
        vec(lambda e: e.tensor_scalar(out=stat[:, 13:14], in0=stat[:, 10:11],
                                      scalar1=stat[:, 10:11], scalar2=None, op0=ALU.mult),
            reads=(b_stat,), writes=(b_stat,))
        vec(lambda e: e.tensor_scalar(out=stat[:, 14:15], in0=stat[:, 11:12],
                                      scalar1=1.0 / C, scalar2=stat[:, 13:14],
                                      op0=ALU.mult, op1=ALU.subtract),
            reads=(b_stat,), writes=(b_stat,))
        comp("scalar", lambda e, ep=eps: e.activation(
            out=stat[:, 15:16], in_=stat[:, 14:15], func=ACT.Ln,
            bias=epsc[:, 0:1] if ep == LN1_EPS else epsc[:, 1:2]),
            reads=(b_stat, b_consts), writes=(b_stat,))
        comp("scalar", lambda e: e.activation(
            out=stat[:, 12:13], in_=stat[:, 15:16], func=ACT.Exp, scale=-0.5),
            reads=(b_stat,), writes=(b_stat,))
        vec(lambda e, u=u_slot, o=out_slot: e.tensor_scalar(
            out=F[o][:], in0=F[u][:], scalar1=stat[:, 10:11], scalar2=stat[:, 12:13],
            op0=ALU.subtract, op1=ALU.mult),
            reads=(b_F[u_slot], b_stat), writes=(b_F[out_slot],))

    def proj_ln_tb(s, tb):
        fslot = tb % 3
        dma("sync", F[fslot][:], x_h[s, tb * 128:(tb + 1) * 128, :], f"f{fslot}", wbuf=b_F[fslot])
        u = 3 + tb % 2
        for hf in range(2):
            pb = (tb * 2 + hf) % 4
            fns = [(lambda e, c=cc, h=hf, p=pb, t=tb: e.matmul(
                PR[p][:], Yf[:, c * T + t * 128: c * T + t * 128 + 128],
                Warena[:, c * C + h * 512: c * C + h * 512 + 512],
                start=(c == 0), stop=(c == 7))) for cc in range(8)]
            group("tensor", fns, reads=(b_Yf, b_Warena), writes=(b_PR[pb],))
            vec(lambda e, h=hf, p=pb, uu=u, fs=fslot: e.scalar_tensor_tensor(
                out=F[uu][:, h * 512:(h + 1) * 512], in0=PR[p][:], scalar=0.0,
                in1=F[fs][:, h * 512:(h + 1) * 512], op0=ALU.add, op1=ALU.add,
                accum_out=stat[:, h:h + 1]),
                reads=(b_PR[pb], b_F[fslot]), writes=(b_F[u], b_stat))
        x1s = 4 + tb % 2
        jslot = 6 + (1 - tb % 2)
        ln_stats_apply(u, LN1_EPS, x1s, jslot)
        k = seq_tb(s, tb)
        dma("gpsimd", x1_d[k * 128:(k + 1) * 128, :], F[x1s][:], "g1",
            rbuf=b_F[x1s], wbuf=b_x1d)
        for hf in range(2):
            vec(lambda e, h=hf, xs=x1s: e.scalar_tensor_tensor(
                out=F[7][:, h * 512:(h + 1) * 512], in0=F[xs][:, h * 512:(h + 1) * 512],
                scalar=0.0, in1=F[xs][:, h * 512:(h + 1) * 512],
                op0=ALU.add, op1=ALU.mult, accum_out=stat[:, 2 + h:3 + h]),
                reads=(b_F[x1s],), writes=(b_F[7], b_stat))
        vec(lambda e, xs=x1s: e.tensor_reduce(out=stat[:, 4:5], in_=F[xs][:],
                                              axis=AX, op=ALU.add),
            reads=(b_F[x1s],), writes=(b_stat,))
        vec(lambda e: e.tensor_scalar(out=stat[:, 4:5], in0=stat[:, 4:5],
                                      scalar1=1.0 / C, scalar2=None, op0=ALU.mult),
            reads=(b_stat,), writes=(b_stat,))
        vec(lambda e: e.tensor_add(out=stat[:, 2:3], in0=stat[:, 2:3], in1=stat[:, 3:4]),
            reads=(b_stat,), writes=(b_stat,))
        vec(lambda e: e.tensor_scalar(out=stat[:, 5:6], in0=stat[:, 4:5],
                                      scalar1=stat[:, 4:5], scalar2=None, op0=ALU.mult),
            reads=(b_stat,), writes=(b_stat,))
        vec(lambda e: e.tensor_scalar(out=stat[:, 6:7], in0=stat[:, 2:3],
                                      scalar1=1.0 / C, scalar2=stat[:, 5:6],
                                      op0=ALU.mult, op1=ALU.subtract),
            reads=(b_stat,), writes=(b_stat,))
        comp("scalar", lambda e: e.activation(out=stat[:, 7:8], in_=stat[:, 6:7],
                                              func=ACT.Ln, bias=epsc[:, 1:2]),
             reads=(b_stat, b_consts), writes=(b_stat,))
        comp("scalar", lambda e: e.activation(
            out=stat[:, 8:9], in_=stat[:, 7:8], func=ACT.Exp, scale=-0.5),
            reads=(b_stat,), writes=(b_stat,))
        vec(lambda e, xs=x1s: e.tensor_scalar(
            out=F[7][:], in0=F[xs][:], scalar1=stat[:, 4:5], scalar2=stat[:, 8:9],
            op0=ALU.subtract, op1=ALU.mult),
            reads=(b_F[x1s], b_stat), writes=(b_F[7],))
        vec(lambda e: e.scalar_tensor_tensor(out=F[7][:], in0=F[7][:], scalar=0.0,
                                             in1=Gb[:], op0=ALU.add, op1=ALU.mult),
            reads=(b_F[7], b_Gb), writes=(b_F[7],))
        vec(lambda e: e.tensor_add(out=F[7][:], in0=F[7][:], in1=Bb[:]),
            reads=(b_F[7], b_Bb), writes=(b_F[7],))
        vec(lambda e: e.tensor_reduce(out=gtmp[:, 0:1], in_=F[7][:], axis=AX,
                                      op=ALU.max, apply_absolute_value=True),
            reads=(b_F[7],), writes=(b_gtmp,))
        vec(lambda e: e.tensor_max(out=gacc[:], in0=gacc[:], in1=gtmp[:, 0:1]),
            reads=(b_gtmp, b_gacc), writes=(b_gacc,))
        dma("gpsimd", xn_d[k * 128:(k + 1) * 128, :], F[7][:], "g3",
            rbuf=b_F[7], wbuf=b_xnd)

    for s in range(NSEQ):
        if s > 0:
            dma("sync", Warena[:], wqv_d[:], "ar", rbuf=b_wqv, wbuf=b_Warena)
        if s == 0:
            u0 = qkv_units(0, 0)
            prep_ob(0, queue="scalar")
            prep_ob(1, queue="scalar")
            for t in range(4):
                u0[t]()
            for k in range(16):
                if k + 2 < 16:
                    prep_ob(k + 2, queue="scalar")
                u0[4 + k]()
        else:
            qkv_tc(s, 0)
        for tc in range(NQC - 1):
            fl = qkv_units(s, tc + 1)
            if s == 0 and tc < 2:
                pu = [(lambda o=ob: prep_ob(o, deferred=True))
                      for ob in range(24 + tc * 8, 32 + tc * 8)]
                fl = [x for pair in zip(fl[:8], pu) for x in pair] + fl[8:]
                if tc == 1:
                    fl.append(lambda: vec(
                        lambda e: e.memset(vt3[:, 12:16, :, 64:65], 1.0),
                        writes=(b_Vt,)))
            attention_qc(s, tc, fl)
        dma("sync", Warena[:], wqp_d[:], "ar", rbuf=b_wqp, wbuf=b_Warena)
        attention_qc(s, NQC - 1,
                     [(lambda t=tb, ss=s: proj_ln_tb(ss, t)) for tb in range(12)])
        for tb in range(12, NTB):
            proj_ln_tb(s, tb)

    comp("tensor", lambda e: e.matmul(RLB[0:1, 0:128], gacc[:], ident_f[:],
                                      is_transpose=True, start=True, stop=True),
         reads=(b_gacc, b_consts), writes=(b_RLB,))
    vec(lambda e: e.tensor_reduce(out=sr(7), in_=RLB[0:1, 0:128], axis=AX, op=ALU.max),
        reads=(b_RLB,), writes=(b_scrow,))
    rec_done = vec(lambda e: e.tensor_copy(out=rec_row[:],
                                           in_=sr(7).broadcast_to([1, 8])),
                   reads=(b_scrow,), writes=(b_peer,))

    def gp_exchange(e):
        par = par_holder["par"]
        for p in (0, 1):
            with e.If_eq(par, p):
                e.dma_start(out=sh[p:p + 1, 0:8], in_=rec_row[:]).then_inc(psem, 16)
        e.wait_ge(psem, 16)
        for p in (0, 1):
            with e.If_eq(par, p):
                e.dma_start(out=sh_u[p:p + 1, 8:16], in_=nz_sb[:]).then_inc(psem, 16)
        e.wait_ge(psem, 32)
        r_tgt = e.alloc_register("tgt")
        r_val = e.alloc_register("val")
        r_non = e.alloc_register("non")
        r_it = e.alloc_register("it")
        e.reg_mov(r_tgt, 32)
        e.reg_mov(r_it, 0)
        e.reg_load(r_non, nz_sb[0:1, 0:1])

        def cond():
            e.reg_add(r_tgt, r_tgt, 16)
            e.reg_add(r_it, r_it, 1)
            for p in (0, 1):
                with e.If_eq(par, p):
                    e.dma_start(out=poll_row[:], in_=sh_u[1 - p:2 - p, 8:16]).then_inc(psem, 16)
            e.wait_ge(psem, r_tgt)
            e.reg_load(r_val, poll_row[0:1, 0:1])
            ne = e.alloc_register("ne")
            lt = e.alloc_register("lt")
            e.reg_alu(ne, r_val, r_non, ALU.not_equal)
            e.reg_alu(lt, r_it, 30000, ALU.is_lt)
            e.reg_alu(ne, ne, lt, ALU.logical_and)
            return ne

        with e.While(cond):
            pass
        for p in (0, 1):
            with e.If_eq(par, p):
                e.dma_start(out=peer_row[:], in_=sh[1 - p:2 - p, 0:8]).then_inc(psem, 16)
        e.reg_add(r_tgt, r_tgt, 16)
        e.wait_ge(psem, r_tgt)
        fl = e.alloc_register("fl")
        e.reg_load(r_val, poll_row[0:1, 0:1])
        e.reg_alu(fl, r_val, r_non, ALU.is_equal)
        e.store(poll_row.bitcast(I32)[0:1, 4:5], fl)
        return e.nop()

    if PAIR_GAMMA:
        ex_op = comp("gpsimd", gp_exchange, reads=(b_nz,), writes=(b_peer,),
                     extra=(rec_done,))
        vec(lambda e: e.tensor_copy(out=gtmp[0:1, 1:2],
                                    in_=poll_row.bitcast(I32)[0:1, 4:5]),
            reads=(b_peer,), writes=(b_gtmp,), extra=(ex_op,))
        vec(lambda e: e.tensor_tensor(out=gtmp[0:1, 2:3], in0=sr(7),
                                      in1=peer_row[0:1, 0:1], op=ALU.max),
            reads=(b_peer, b_scrow), writes=(b_gtmp,), extra=(ex_op,))
        vec(lambda e: e.tensor_sub(out=gtmp[0:1, 2:3], in0=gtmp[0:1, 2:3], in1=sr(7)),
            reads=(b_scrow, b_gtmp), writes=(b_gtmp,))
        vec(lambda e: e.scalar_tensor_tensor(out=gtmp[0:1, 2:3], in0=gtmp[0:1, 2:3],
                                             scalar=gtmp[0:1, 1:2], in1=sr(7),
                                             op0=ALU.mult, op1=ALU.add),
            reads=(b_gtmp, b_scrow), writes=(b_gtmp,))
        vec(lambda e: e.tensor_copy(out=sr(7), in_=gtmp[0:1, 2:3]),
            reads=(b_gtmp,), writes=(b_scrow,))
    vec(lambda e: e.reciprocal(out=sr(8), in_=sr(7)), reads=(b_scrow,), writes=(b_scrow,))
    vec(lambda e: e.tensor_scalar(out=sr(8), in0=sr(8), scalar1=128.0, scalar2=None,
                                  op0=ALU.mult),
        reads=(b_scrow,), writes=(b_scrow,))
    vec(lambda e: e.tensor_scalar(out=sr(9), in0=sr(7), scalar1=sr(6),
                                  scalar2=1.0 / 128.0, op0=ALU.mult, op1=ALU.mult),
        reads=(b_scrow,), writes=(b_scrow,))
    bcast_scrow()
    vec(lambda e: e.tensor_scalar(out=Bb[:], in0=Bb[:], scalar1=scl[:, 8:9],
                                  scalar2=None, op0=ALU.mult),
        reads=(b_Bb, b_scl), writes=(b_Bb,))
    dma("sync", Warena[:], wqf_d[:], "ar", rbuf=b_wqf, wbuf=b_Warena)

    def p2_front(k):
        xns, x1s = (k % 2) * 2, (k % 2) * 2 + 1
        xq = k % 2
        dma("sync", F[xns][:], xn_d[k * 128:(k + 1) * 128, :], f"f{xns}",
            rbuf=b_xnd, wbuf=b_F[xns])
        dma("scalar", F[x1s][:], x1_d[k * 128:(k + 1) * 128, :], f"f{x1s}",
            rbuf=b_x1d, wbuf=b_F[x1s])
        vec(lambda e, q=xq, xs=xns: e.tensor_scalar(
            out=xq_tm[q][:], in0=F[xs][:], scalar1=scl[:, 8:9], scalar2=CLIP,
            op0=ALU.mult, op1=ALU.min),
            reads=(b_F[xns], b_scl), writes=(b_Xbf[0],))
        vec(lambda e, q=xq: e.tensor_scalar(
            out=xq_tm[q][:], in0=xq_tm[q][:], scalar1=-CLIP, scalar2=None,
            op0=ALU.max),
            reads=(b_Xbf[0],), writes=(b_Xbf[0],))
        transpose8(xq_tm[xq], (b_Xbf[0],))
        vec(lambda e, q=xq: e.tensor_copy(out=xq_fm[q][:], in_=PSB[:]),
            reads=(b_PSB,), writes=(b_Xbf[1],))
        u = 4 + k % 2
        for hf in range(2):
            pb = (k * 2 + hf) % 4
            fns = [(lambda e, c=cc, h=hf, p=pb, q=xq: e.matmul(
                PR[p][:], xq_fm[q][:, c * 128:(c + 1) * 128],
                Warena[:, c * C + h * 512: c * C + h * 512 + 512],
                start=(c == 0), stop=(c == 7))) for cc in range(8)]
            group("tensor", fns, reads=(b_Xbf[1], b_Warena), writes=(b_PR[pb],))

    def p2_back(k):
        s, tb = divmod(k, NTB)
        x1s = (k % 2) * 2 + 1
        u = 4 + k % 2
        outs = 6 + k % 2
        for hf in range(2):
            pb = (k * 2 + hf) % 4
            vec(lambda e, h=hf, p=pb, uu=u, fs=x1s: e.scalar_tensor_tensor(
                out=F[uu][:, h * 512:(h + 1) * 512], in0=PR[p][:],
                scalar=scl[:, 9:10], in1=F[fs][:, h * 512:(h + 1) * 512],
                op0=ALU.mult, op1=ALU.add, accum_out=stat[:, h:h + 1]),
                reads=(b_PR[pb], b_F[x1s], b_scl), writes=(b_F[u], b_stat))
        ln_stats_apply(u, LN2_EPS, outs, outs)
        dma("gpsimd", out_h[s, tb * 128:(tb + 1) * 128, :], F[outs][:], "g2",
            rbuf=b_F[outs])

    NK = NSEQ * NTB
    p2_front(0)
    for i in range(NK):
        if i + 1 < NK:
            p2_front(i + 1)
        p2_back(i)

    S.finalize()
    finals = {}
    for op in S.ops["gpsimd"]:
        if op.val is not None and op.val[0] in ("g1", "g2", "g3", "sk", "sp", "sf", "sv"):
            finals[op.val[0]] = max(finals.get(op.val[0], 0), op.val[1])

    with nc.Block() as block:
        lp = lambda: nc.allow_low_precision(
            "bf16 matmul operands with fp32 accumulation (sanctioned); "
            "all normalisation math is fp32")

        @block.sync
        def _(e):
            with lp():
                S.replay("sync", e, sems)

        @block.scalar
        def _(e):
            with lp():
                S.replay("scalar", e, sems)

        @block.tensor
        def _(e):
            with lp():
                S.replay("tensor", e, sems)

        @block.vector
        def _(e):
            with lp():
                S.replay("vector", e, sems)

        @block.gpsimd
        def _(e):
            with lp():
                S.replay("gpsimd", e, sems)
            for n in sorted(finals):
                e.wait_ge(sems[n], finals[n])

    ctx.close()
    return nc


_cache = {}


def kernel(L0_attn_x, L0_attn_w_attn, L0_attn_w_proj, L0_ffn_weight,
           L0_ffn_ln_gamma, L0_ffn_ln_beta):
    x = np.ascontiguousarray(np.asarray(L0_attn_x, np.float32))
    wa = np.ascontiguousarray(np.asarray(L0_attn_w_attn, np.float32))
    wp = np.ascontiguousarray(np.asarray(L0_attn_w_proj, np.float32))
    wf = np.ascontiguousarray(np.asarray(L0_ffn_weight, np.float32))
    g = np.ascontiguousarray(np.asarray(L0_ffn_ln_gamma, np.float32)).reshape(1, C)
    b = np.ascontiguousarray(np.asarray(L0_ffn_ln_beta, np.float32)).reshape(1, C)
    nonce = np.full((1, 8), np.uint32((time.time_ns() >> 8) & 0x7FFFFFFF) | np.uint32(1),
                    dtype=np.uint32)
    if "nc" not in _cache:
        _cache["nc"] = build_nc()
    nc = _cache["nc"]
    shards = x.reshape(N_CORES, NSEQ, T, C)
    in_maps = [{"x": shards[i], "w_attn": wa, "w_proj": wp, "w_ffn": wf,
                "lng": g, "lnb": b, "nonce": nonce} for i in range(N_CORES)]
    res = run_bass_kernel_spmd(nc, in_maps, core_ids=list(range(N_CORES)))
    out = np.concatenate([r["out"].reshape(NSEQ, T, C) for r in res.results], axis=0)
    return out


# version 29
# speedup vs baseline: 1.0309x; 1.0309x over previous
import time
import numpy as np
import ml_dtypes
import concourse.bass as bass
import concourse.mybir as mybir
from concourse.bass_utils import run_bass_kernel_spmd

F32 = mybir.dt.float32
BF16 = mybir.dt.bfloat16
U32 = mybir.dt.uint32
I32 = mybir.dt.int32
ALU = mybir.AluOpType
ACT = mybir.ActivationFunctionType
AX = mybir.AxisListType.X

N_CORES = 8
NSEQ = 2
T = 2048
C = 1024
H = 16
NTB = T // 128
NQC = T // 512
CLIP = 128.0 - 1e-5
LN1_EPS = 1e-5
LN2_EPS = 1e-5
FFN_LN_EPS = 1e-3
PAIR_GAMMA = False

ENGS = ("sync", "scalar", "tensor", "vector", "gpsimd")
ESEM = {"sync": ("sy", 16), "scalar": ("sc", 1), "tensor": ("pe", 1),
        "vector": ("dv", 1), "gpsimd": ("gp", 1)}


class Op:
    __slots__ = ("eng", "fn", "waits", "sem", "needed", "val")

    def __init__(self, eng, fn, waits, sem):
        self.eng = eng
        self.fn = fn
        self.waits = [(w, r) for (w, r) in waits if w is not None]
        self.sem = sem
        self.needed = False
        self.val = None


class Sched:

    def __init__(self):
        self.ops = {e: [] for e in ENGS}
        self.order = []

    def emit(self, eng, fn, waits=(), dma=None):
        sem = (dma, 16) if dma else ESEM[eng]
        op = Op(eng, fn, waits, sem)
        if dma:
            op.needed = True
        self.ops[eng].append(op)
        self.order.append(op)
        return op

    @staticmethod
    def _edge_kept(op, w, raw):
        if w.eng != op.eng or w.sem[1] == 16:
            return True
        return raw and op.eng != "tensor"

    def finalize(self):
        for op in self.order:
            for w, raw in op.waits:
                if self._edge_kept(op, w, raw):
                    w.needed = True
        cnt = {}
        for op in self.order:
            if op.needed:
                name, inc = op.sem
                cnt[name] = cnt.get(name, 0) + inc
                op.val = (name, cnt[name])

    def replay(self, eng_name, eng, sems):
        waited = {}
        for op in self.ops[eng_name]:
            tgt = {}
            for w, raw in op.waits:
                if not self._edge_kept(op, w, raw):
                    continue
                if w.val is None:
                    continue
                n, v = w.val
                if v > tgt.get(n, -1):
                    tgt[n] = v
            for n, v in sorted(tgt.items()):
                if v > waited.get(n, 0):
                    eng.wait_ge(sems[n], v)
                    waited[n] = v
            inst = op.fn(eng)
            if op.needed and op.val is not None and inst is not None:
                inst.then_inc(sems[op.val[0]], op.sem[1])


class Buf:
    __slots__ = ("w", "r")

    def __init__(self):
        self.w = None
        self.r = []

    def rd_waits(self):
        return [self.w] if self.w is not None else []

    def wr_waits(self):
        return list(self.r) + ([self.w] if self.w is not None else [])

    def rd(self, op):
        self.r.append(op)

    def wrote(self, op):
        self.w = op
        self.r = []


def build_nc():
    nc = bass.Bass("TRN2", target_bir_lowering=False, detect_race_conditions=False)

    x_h = nc.dram_tensor("x", [NSEQ, T, C], F32, kind="ExternalInput").ap()
    wa_h = nc.dram_tensor("w_attn", [3 * C, C], F32, kind="ExternalInput").ap()
    wp_h = nc.dram_tensor("w_proj", [C, C], F32, kind="ExternalInput").ap()
    wf_h = nc.dram_tensor("w_ffn", [C, C], F32, kind="ExternalInput").ap()
    g_h = nc.dram_tensor("lng", [1, C], F32, kind="ExternalInput").ap()
    b_h = nc.dram_tensor("lnb", [1, C], F32, kind="ExternalInput").ap()
    nz_h = nc.dram_tensor("nonce", [1, 8], U32, kind="ExternalInput").ap()
    out_h = nc.dram_tensor("out", [NSEQ, T, C], F32, kind="ExternalOutput").ap()

    x1_d = nc.dram_tensor("x1_spill", [NSEQ * T, C], F32, kind="Internal").ap()
    xn_d = nc.dram_tensor("xn_spill", [NSEQ * T, C], F32, kind="Internal").ap()
    wqk_d = nc.dram_tensor("wq_qk", [128, 16 * C], BF16, kind="Internal").ap()
    wqv_d = nc.dram_tensor("wq_v", [128, 8 * C], BF16, kind="Internal").ap()
    wqp_d = nc.dram_tensor("wq_p", [128, 8 * C], BF16, kind="Internal").ap()
    wqf_d = nc.dram_tensor("wq_f", [128, 8 * C], BF16, kind="Internal").ap()
    sh = nc.dram_tensor("sh_pair", [2, 16], F32, kind="Internal", addr_space="Shared").ap()
    sh_u = sh.bitcast(U32)

    ident_bf_h = nc.inline_tensor(np.eye(128, dtype=ml_dtypes.bfloat16), "c_ident_bf").ap()
    ident_f_h = nc.inline_tensor(np.eye(128, dtype=np.float32), "c_ident_f").ap()
    mk = np.tril(np.ones((128, 128), np.float32)).T.astype(ml_dtypes.bfloat16)
    mask_h = nc.inline_tensor(mk, "causal_mask").ap()

    from contextlib import ExitStack
    ctx = ExitStack()
    sb = lambda n, s, d: ctx.enter_context(nc.sbuf_tensor(n, s, d))
    psumt = lambda n, s, d=F32: ctx.enter_context(nc.psum_tensor(n, s, d))

    budget = [0]

    def sbt(name, shape, dtype):
        per = int(np.prod(shape[1:])) * (2 if dtype == BF16 else 4)
        budget[0] += per
        return sb(name, shape, dtype)

    ident_bf = sbt("ident_bf", [128, 128], BF16)
    ident_f = sbt("ident_f", [128, 128], F32)
    mask_bf = sbt("mask_bf", [128, 128], BF16)
    ones_col = sbt("ones_col", [1, 128], F32)
    ones_128 = sbt("ones_128", [128, 1], F32)
    ones_rl = sbt("ones_rl", [128, 64], BF16)
    Gb = sbt("G_bcast", [128, C], F32)
    Bb = sbt("B_bcast", [128, C], F32)
    scl = sbt("scl", [128, 16], F32)
    sc_row = sbt("sc_row", [1, 16], F32)
    rec_row = sbt("rec_row", [1, 8], F32)
    peer_row = sbt("peer_row", [1, 8], F32)
    poll_row = sbt("poll_row", [1, 8], U32)
    nz_sb = sbt("nz_sb", [1, 8], U32)
    gacc = sbt("gacc", [128, 1], F32)
    gtmp = sbt("gtmp", [128, 4], F32)
    stat = sbt("stat", [128, 16], F32)
    wsum = sbt("wsum", [128, 40], F32)
    epsc = sbt("epsc", [128, 2], F32)
    Wqk_r = [sbt(f"Wqkr{i}", [128, C], BF16) for i in range(2)]
    Warena = sbt("Warena", [128, 8 * C], BF16)
    Kt = sbt("Kt", [128, 8 * T], BF16)
    Vt = sbt("Vt", [128, NTB * 1040], BF16)
    Yf = sbt("Yf", [128, 8 * T], BF16)
    Qr = [sbt(f"Qr{i}", [128, 8 * 512], BF16) for i in range(2)]
    Xbf = [sbt(f"Xbf{i}", [128, 8 * 512], BF16) for i in range(2)]
    Pt = [sbt(f"Pt{i}", [128, 512], BF16) for i in range(6)]
    xtmp = [sbt(f"xtmp{i}", [128, C], BF16) for i in range(2)]
    F = [sbt(f"F{i}", [128, C], F32) for i in range(8)]
    rl_f = sbt("rl_f", [65, 512], F32)
    rl_bf = [sbt(f"rl_bf{i}", [65, 512], BF16) for i in range(2)]

    assert budget[0] <= 211000, f"SBUF budget: {budget[0]} B/partition"

    xq_tm = [Xbf[0][:, 0:C], Xbf[0][:, C:2 * C]]
    xq_fm = [Xbf[1][:, 0:C], Xbf[1][:, C:2 * C]]
    wq_st = [Yf[:, i * C:(i + 1) * C] for i in range(3)]
    sg_st = [Yf[:, (3 + i) * C:(4 + i) * C] for i in range(3)]
    tr_st = [Yf[:, (6 + i) * C:(7 + i) * C] for i in range(2)]

    PB = [psumt(f"PB{i}", [128, 512]) for i in range(2)]
    YB = [psumt(f"YB{i}", [65, 512]) for i in range(2)]
    RLB = psumt("RLB", [64, 512])
    PSB = psumt("PSB", [128, 1024], BF16)
    SPB = [psumt(f"SPB{i}", [128, 512]) for i in range(2)]
    PR = [PB[0], PB[1], SPB[0], SPB[1]]

    sems = {n: nc.alloc_semaphore(name=n) for n in
            ("sc", "pe", "dv", "gp", "cn0", "cn1", "cn2", "cn3", "f0", "f1",
             "f2", "f3", "f4", "f5", "f6", "f7", "wk0", "wk1", "ar", "sk",
             "sp", "sf", "sv", "g1", "g2", "g3")}
    psem = nc.alloc_semaphore(name="poll")

    S = Sched()

    b_consts = Buf()
    b_Gb, b_Bb, b_scl, b_scrow = Buf(), Buf(), Buf(), Buf()
    b_wsum = Buf()
    b_Wqk_r = [Buf(), Buf()]
    b_Warena = Buf()
    b_Kt, b_Vt, b_Yf = Buf(), Buf(), Buf()
    b_Qr = [Buf(), Buf()]
    b_Xbf = [Buf(), Buf()]
    b_Pt = [Buf() for _ in range(6)]
    b_xtmp = [Buf(), Buf()]
    b_F = [Buf() for _ in range(8)]
    b_PB = [Buf(), Buf()]
    b_YB = [Buf(), Buf()]
    b_RLB, b_PSB = Buf(), Buf()
    b_SPB = [Buf(), Buf()]
    b_PR = [b_PB[0], b_PB[1], b_SPB[0], b_SPB[1]]
    b_rlf = Buf()
    b_rlbf = [Buf(), Buf()]
    b_gacc, b_gtmp, b_stat = Buf(), Buf(), Buf()
    b_x1d = Buf()
    b_xnd = Buf()
    b_wqk, b_wqv, b_wqp, b_wqf = Buf(), Buf(), Buf(), Buf()
    b_nz, b_peer = Buf(), Buf()

    def dma(eng, out_ap, in_ap, sem, rbuf=None, wbuf=None, extra=()):
        waits = [(w, True) for w in extra]
        if rbuf is not None:
            waits += [(w, True) for w in rbuf.rd_waits()]
        if wbuf is not None:
            waits += [(w, False) for w in wbuf.wr_waits()]
        op = S.emit(eng, lambda e, o=out_ap, i=in_ap: e.dma_start(out=o, in_=i),
                    waits, dma=sem)
        if rbuf is not None:
            rbuf.rd(op)
        if wbuf is not None:
            wbuf.wrote(op)
        return op

    def comp(eng, fn, reads=(), writes=(), extra=()):
        waits = [(w, True) for w in extra]
        for b in reads:
            waits += [(w, True) for w in b.rd_waits()]
        for b in writes:
            waits += [(w, False) for w in b.wr_waits()]
        op = S.emit(eng, fn, waits)
        for b in reads:
            b.rd(op)
        for b in writes:
            b.wrote(op)
        return op

    def vec(fn, reads=(), writes=(), extra=()):
        return comp("vector", fn, reads, writes, extra)

    def group(eng, fns, reads=(), writes=(), extra=()):
        waits = [(w, True) for w in extra]
        for b in reads:
            waits += [(w, True) for w in b.rd_waits()]
        for b in writes:
            waits += [(w, False) for w in b.wr_waits()]
        ops = []
        for j, fn in enumerate(fns):
            ops.append(S.emit(eng, fn, waits if j == 0 else [(ops[-1], False)]))
        for b in reads:
            b.rd(ops[-1])
        for b in writes:
            b.wrote(ops[-1])
        return ops[-1]

    dma("sync", ident_bf[:], ident_bf_h[:], "cn0", wbuf=b_consts)
    dma("sync", ident_f[:], ident_f_h[:], "cn1", wbuf=b_consts)
    dma("sync", mask_bf[:], mask_h[:], "cn2", wbuf=b_consts)
    dma("sync", nz_sb[:], nz_h[:], "cn3", wbuf=b_nz)
    vec(lambda e: e.memset(ones_col[:], 1.0), writes=(b_consts,))
    vec(lambda e: e.memset(ones_128[:], 1.0), writes=(b_consts,))
    vec(lambda e: e.memset(ones_rl[:], 1.0), writes=(b_consts,))
    vec(lambda e: e.memset(gacc[:], 0.0), writes=(b_gacc,))
    vec(lambda e: e.memset(sc_row[:], 0.0), writes=(b_scrow,))
    vec(lambda e: e.memset(epsc[:, 0:1], LN1_EPS), writes=(b_consts,))
    vec(lambda e: e.memset(epsc[:, 1:2], FFN_LN_EPS), writes=(b_consts,))
    vt3 = Vt[:].rearrange("p (tb h f) -> p tb h f", tb=NTB, h=H, f=65)
    vec(lambda e: e.memset(vt3[:, :, :, 64:65], 1.0), writes=(b_Vt,))

    dma("sync", F[0][0:1, :], g_h[:], "f0", wbuf=b_F[0])
    dma("sync", F[1][0:1, :], b_h[:], "f1", wbuf=b_F[1])
    for src, bbuf, dst, bdst in ((F[0], b_F[0], Gb, b_Gb), (F[1], b_F[1], Bb, b_Bb)):
        for hf in range(2):
            comp("tensor", lambda e, s=src, h=hf: e.matmul(
                SPB[0][:, 0:512], ones_col[:], s[0:1, h * 512:(h + 1) * 512],
                start=True, stop=True),
                reads=(bbuf, b_consts), writes=(b_SPB[0],))
            vec(lambda e, h=hf, d=dst: e.tensor_copy(
                out=d[:, h * 512:(h + 1) * 512], in_=SPB[0][:, 0:512]),
                reads=(b_SPB[0],), writes=(bdst,))

    def wsrc(ob):
        if ob < 24:
            return wa_h[ob * 128:(ob + 1) * 128, :]
        if ob < 32:
            return wp_h[(ob - 24) * 128:(ob - 23) * 128, :]
        return wf_h[(ob - 32) * 128:(ob - 31) * 128, :]

    for ob in range(40):
        slot = ob % 3
        dma("sync" if ob % 2 == 0 else "scalar", F[slot][:], wsrc(ob),
            f"f{slot}", wbuf=b_F[slot])
        vec(lambda e, s=slot, o=ob: e.tensor_reduce(
            out=wsum[:, o:o + 1], in_=F[s][:], axis=AX, op=ALU.add,
            apply_absolute_value=True),
            reads=(b_F[slot],), writes=(b_wsum,))
    comp("tensor", lambda e: e.matmul(SPB[0][0:1, 0:40], ones_128[:], wsum[:, 0:40],
                                      start=True, stop=True),
         reads=(b_wsum, b_consts), writes=(b_SPB[0],))

    def sr(c):
        return sc_row[0:1, c:c + 1]

    vec(lambda e: e.tensor_reduce(out=sr(2), in_=SPB[0][0:1, 0:24], axis=AX, op=ALU.add),
        reads=(b_SPB[0],), writes=(b_scrow,))
    vec(lambda e: e.tensor_reduce(out=sr(4), in_=SPB[0][0:1, 24:32], axis=AX, op=ALU.add),
        reads=(b_SPB[0],), writes=(b_scrow,))
    vec(lambda e: e.tensor_reduce(out=sr(6), in_=SPB[0][0:1, 32:40], axis=AX, op=ALU.add),
        reads=(b_SPB[0],), writes=(b_scrow,))
    n_attn, n_pf = float(24 * 128 * 1024), float(8 * 128 * 1024)
    vec(lambda e: e.tensor_scalar(out=sr(2), in0=sr(2), scalar1=1.0 / n_attn,
                                  scalar2=1e-8, op0=ALU.mult, op1=ALU.max),
        reads=(b_scrow,), writes=(b_scrow,))
    vec(lambda e: e.tensor_scalar(out=sr(4), in0=sr(4), scalar1=1.0 / n_pf,
                                  scalar2=1e-8, op0=ALU.mult, op1=ALU.max),
        reads=(b_scrow,), writes=(b_scrow,))
    vec(lambda e: e.tensor_scalar(out=sr(6), in0=sr(6), scalar1=1.0 / n_pf,
                                  scalar2=None, op0=ALU.mult),
        reads=(b_scrow,), writes=(b_scrow,))
    vec(lambda e: e.reciprocal(out=sr(0), in_=sr(2)), reads=(b_scrow,), writes=(b_scrow,))
    vec(lambda e: e.reciprocal(out=sr(3), in_=sr(4)), reads=(b_scrow,), writes=(b_scrow,))
    vec(lambda e: e.tensor_scalar(out=sr(15), in0=sr(6), scalar1=1e-5, scalar2=None,
                                  op0=ALU.add),
        reads=(b_scrow,), writes=(b_scrow,))
    vec(lambda e: e.reciprocal(out=sr(5), in_=sr(15)), reads=(b_scrow,), writes=(b_scrow,))
    vec(lambda e: e.tensor_scalar(out=sr(1), in0=sr(2), scalar1=sr(2), scalar2=0.125,
                                  op0=ALU.mult, op1=ALU.mult),
        reads=(b_scrow,), writes=(b_scrow,))
    comp("scalar", lambda e: e.activation(out=sr(10), in_=sr(4), func=ACT.Ln),
         reads=(b_scrow,), writes=(b_scrow,))

    def bcast_scrow():
        comp("tensor", lambda e: e.matmul(SPB[0][:, 0:16], ones_col[:], sc_row[:],
                                          start=True, stop=True),
             reads=(b_scrow, b_consts), writes=(b_SPB[0],))
        vec(lambda e: e.tensor_copy(out=scl[:], in_=SPB[0][:, 0:16]),
            reads=(b_SPB[0],), writes=(b_scl,))

    bcast_scrow()

    psb3 = PSB[:].rearrange("p (ch t) -> p ch t", ch=8, t=128)

    def transpose8(src_ap, rbufs):
        fns = [(lambda e, j=i, sp=src_ap: e.matmul(
            PSB[:, j * 128:(j + 1) * 128], sp[:, j * 128:(j + 1) * 128], ident_bf[:],
            is_transpose=True, start=(j == 0), stop=(j == 7))) for i in range(8)]
        return group("tensor", fns, reads=tuple(rbufs) + (b_consts,), writes=(b_PSB,))

    vtail = lambda i: Vt[:, 12480 + i * 1040:12480 + i * 1040 + C]

    def prep_ob(ob, deferred=False, queue=None):
        slot, qs = ob % 2, ob % 3
        dma(queue or ("sync" if ob % 2 == 0 else "scalar"), F[slot][:], wsrc(ob),
            f"f{slot}", wbuf=b_F[slot])
        sc_col = scl[:, 0:1] if ob < 24 else (scl[:, 3:4] if ob < 32 else scl[:, 5:6])
        a, q = 2 + qs, 5 + qs
        if deferred:
            wq_t, sg_t, tr_t, b_st = vtail(0), vtail(1), vtail(2), b_Vt
        else:
            wq_t, sg_t, tr_t, b_st = wq_st[qs], sg_st[qs], tr_st[ob % 2], b_Yf
        comp("scalar", lambda e, s=slot, aa=a, c=sc_col: e.activation(
            out=F[aa][:], in_=F[s][:], func=ACT.Abs, scale=c),
            reads=(b_F[slot], b_scl), writes=(b_F[a],))
        comp("scalar", lambda e, s=slot, o=sg_t: e.sign(out=o, in_=F[s][:]),
             reads=(b_F[slot],), writes=(b_st,))
        vec(lambda e, aa=a, qq=q: e.tensor_scalar(
            out=F[qq][:], in0=F[aa][:], scalar1=0.5, scalar2=None, op0=ALU.is_ge),
            reads=(b_F[a],), writes=(b_F[q],))
        if ob < 32:
            vec(lambda e, aa=a, qq=q: e.scalar_tensor_tensor(
                out=F[qq][:], in0=F[aa][:], scalar=1.5, in1=F[qq][:],
                op0=ALU.is_ge, op1=ALU.add),
                reads=(b_F[a], b_F[q]), writes=(b_F[q],))
        vec(lambda e, qq=q, wo=wq_t, so=sg_t: e.tensor_tensor(
            out=wo, in0=F[qq][:], in1=so, op=ALU.mult),
            reads=(b_F[q], b_st), writes=(b_st,))

        if ob >= 32:
            ch = ob - 32
            dma("gpsimd", wqf_d[:, ch * C:(ch + 1) * C], wq_t, "sf",
                rbuf=b_st, wbuf=b_wqf)
            return
        transpose8(wq_t, (b_st,))
        tr = tr_t
        if ob < 16:
            vec(lambda e, o=tr: e.tensor_copy(out=o, in_=psb3[:].rearrange("p a b -> p (a b)")),
                reads=(b_PSB,), writes=(b_st,))
            dma("gpsimd", wqk_d[:, ob * C:(ob + 1) * C], tr, "sk", rbuf=b_st, wbuf=b_wqk)
        elif ob < 24:
            wa3 = Warena[:].rearrange("p (ch v) -> p ch v", ch=8, v=C)
            vec(lambda e, o=ob - 16: e.tensor_copy(
                out=wa3[:, :, o * 128:(o + 1) * 128], in_=psb3[:]),
                reads=(b_PSB,), writes=(b_Warena,))
        else:
            vec(lambda e, o=tr: e.tensor_copy(out=o, in_=psb3[:].rearrange("p a b -> p (a b)")),
                reads=(b_PSB,), writes=(b_st,))
            o = ob - 24
            wq3d = wqp_d[:].rearrange("p (ch v) -> p ch v", ch=8, v=C)
            st3 = tr.rearrange("p (ch t) -> p ch t", ch=8, t=128)
            dma("gpsimd", wq3d[:, :, o * 128:(o + 1) * 128], st3[:], "sp", rbuf=b_st, wbuf=b_wqp)

    for ob in range(16, 24):
        prep_ob(ob)
    dma("gpsimd", wqv_d[:], Warena[:], "sv", rbuf=b_Warena, wbuf=b_wqv)
    for ob in range(16):
        prep_ob(ob)

    par_holder = {}

    def gp_pre(e):
        par = e.alloc_register("par")
        for arm in e.switch_core_id(n=128):
            e.reg_mov(par, int(arm) % 2)
        par_holder["par"] = par
        return None

    S.emit("gpsimd", gp_pre, ())

    seq_tb = lambda s, tb: s * NTB + tb

    def qkv_units(s, tc):
        xs = tc % 2
        xb3 = Xbf[xs][:].rearrange("p (ch t) -> p ch t", ch=8, t=512)
        units = []

        def unit_tbl(tbl):
                tb = tc * 4 + tbl
                fslot, xt = tb % 3, tb % 2
                dma("sync", F[fslot][:], x_h[s, tb * 128:(tb + 1) * 128, :], f"f{fslot}", wbuf=b_F[fslot])
                comp("scalar", lambda e, fs=fslot, x=xt: e.activation(
                    out=xtmp[x][:], in_=F[fs][:], func=ACT.Copy),
                    reads=(b_F[fslot],), writes=(b_xtmp[xt],))
                transpose8(xtmp[xt], (b_xtmp[xt],))
                vec(lambda e, t=tbl: e.tensor_copy(
                    out=xb3[:, :, t * 128:(t + 1) * 128], in_=psb3[:]),
                    reads=(b_PSB,), writes=(b_Xbf[xs],))
                for hf in range(2):
                    pb = (tb * 2 + hf) % 4
                    fns = [(lambda e, c=cc, h=hf, t=tbl, p=pb: e.matmul(
                        PR[p][:], xb3[:, c, t * 128:(t + 1) * 128],
                        Warena[:, c * C + h * 512:c * C + h * 512 + 512],
                        start=(c == 0), stop=(c == 7))) for cc in range(8)]
                    group("tensor", fns, reads=(b_Xbf[xs], b_Warena), writes=(b_PR[pb],))
                    vo = vt3[:, tb, hf * 8:(hf + 1) * 8, 0:64]
                    vec(lambda e, o=vo, p=pb: e.tensor_scalar(
                        out=o, in0=PR[p][:], scalar1=scl[:, 2:3], scalar2=None, op0=ALU.mult),
                        reads=(b_PR[pb], b_scl), writes=(b_Vt,))

        def unit_ob(ob):
                wr = ob % 2
                dma("sync", Wqk_r[wr][:], wqk_d[:, ob * C:(ob + 1) * C], f"wk{wr}",
                    rbuf=b_wqk, wbuf=b_Wqk_r[wr])
                pb = ob % 4
                fns = [(lambda e, c=cc, w=wr, p=pb: e.matmul(
                    PR[p][:], Wqk_r[w][:, c * 128:(c + 1) * 128],
                    Xbf[xs][:, c * 512:(c + 1) * 512],
                    start=(c == 0), stop=(c == 7))) for cc in range(8)]
                group("tensor", fns, reads=(b_Wqk_r[wr], b_Xbf[xs]), writes=(b_PR[pb],))
                if ob < 8:
                    vec(lambda e, o=ob, p=pb: e.tensor_scalar(
                        out=Qr[tc % 2][:, o * 512:(o + 1) * 512], in0=PR[p][:],
                        scalar1=scl[:, 1:2], scalar2=None, op0=ALU.mult),
                        reads=(b_PR[pb], b_scl), writes=(b_Qr[tc % 2],))
                else:
                    vec(lambda e, o=ob - 8, p=pb: e.tensor_copy(
                        out=Kt[:, o * T + tc * 512: o * T + (tc + 1) * 512], in_=PR[p][:]),
                        reads=(b_PR[pb],), writes=(b_Kt,))

        for tbl in range(4):
            units.append(lambda t=tbl: unit_tbl(t))
        for ob in range(16):
            units.append(lambda o=ob: unit_ob(o))
        return units

    def qkv_tc(s, tc):
        for u in qkv_units(s, tc):
            u()

    attn_gate = [None]

    def attention_qc(s, qc, fills=()):
        qs = qc % 2
        nkb = 4 * qc + 4
        sring, bring = PR, b_PR

        def head_stream(h, exp_gate_in):
            yb = (qc * H + h) % 2
            chunk, prow = h // 2, (h % 2) * 64
            exp_ops = {}
            prod_ops = {}

            def geom(kb):
                return (kb - 4 * qc) * 128 if kb >= 4 * qc else 0

            def mm_S(kb, gate):
                qlo, r = geom(kb), kb % 4
                waits = ([(gate, True)] if gate is not None else [])
                waits += [(w, False) for w in bring[r].wr_waits()]
                op = S.emit("tensor", lambda e, k=kb, q=qlo, rr=r: e.matmul(
                    sring[rr][:, q:512],
                    Kt[prow:prow + 64, chunk * T + k * 128: chunk * T + (k + 1) * 128],
                    Qr[qs][prow:prow + 64, chunk * 512 + q:(chunk + 1) * 512],
                    start=True, stop=True),
                    waits)
                b_Kt.rd(op)
                b_Qr[qs].rd(op)
                bring[r].wrote(op)
                return op

            def do_exp(kb, gate):
                qlo, r, pr = geom(kb), kb % 4, kb % 6
                op = S.emit("scalar", lambda e, q=qlo, rr=r, p=pr: e.activation(
                    out=Pt[p][:, q:512], in_=sring[rr][:, q:512], func=ACT.Exp),
                    [(gate, True)] if gate is not None else [])
                op.needed = True
                bring[r].rd(op)
                b_Pt[pr].wrote(op)
                exp_ops[kb] = op
                if kb >= 4 * qc:
                    mop = S.emit("vector", lambda e, q=qlo, p=pr: e.tensor_tensor(
                        out=Pt[p][:, q:q + 128], in0=Pt[p][:, q:q + 128],
                        in1=mask_bf[:], op=ALU.mult), [(op, True)])
                    mop.needed = True
                    b_consts.rd(mop)
                    b_Pt[pr].wrote(mop)
                    prod_ops[kb] = mop
                else:
                    prod_ops[kb] = op
                return op

            def mm_AV(kb, gate):
                qlo, pr = geom(kb), kb % 6
                fn = lambda e, k=kb, q=qlo, p=pr: e.matmul(
                    YB[yb][:, q:512], Vt[:, k * 1040 + h * 65: k * 1040 + (h + 1) * 65],
                    Pt[p][:, q:512], start=(k == 0), stop=(k == nkb - 1))
                waits = [(gate, True)]
                if kb == 0:
                    waits += [(w, False) for w in b_YB[yb].wr_waits()]
                else:
                    waits += [(b_YB[yb].w, False)]
                op = S.emit("tensor", fn, waits)
                b_Vt.rd(op)
                b_Pt[pr].rd(op)
                b_YB[yb].wrote(op) if kb == nkb - 1 else b_YB[yb].__setattr__("w", op)

            for j in range(nkb // 2):
                a, b = 2 * j, 2 * j + 1
                gate = exp_ops.get(a - 3, exp_gate_in if a < 4 else None)
                mm_S(a, gate)
                sgb = mm_S(b, None)
                do_exp(a, sgb)
                do_exp(b, None)
                if a >= 4:
                    mm_AV(a - 4, prod_ops[a - 4])
                    mm_AV(b - 4, prod_ops[a - 3])
            for kk in range(max(0, nkb - 4), nkb):
                mm_AV(kk, prod_ops[kk])
            return yb, exp_ops[nkb - 1]

        def tail_a(h, yb):
            rb = h % 2
            comp("scalar", lambda e, y=yb: e.activation(
                out=rl_f[64:65, :], in_=YB[y][64:65, :], func=ACT.Ln),
                reads=(b_YB[yb],), writes=(b_rlf,))
            comp("scalar", lambda e, r=rb: e.activation(
                out=rl_bf[r][64:65, :], in_=rl_f[64:65, :], func=ACT.Exp,
                scale=-1.0, bias=scl[64:65, 10:11]),
                reads=(b_rlf, b_scl), writes=(b_rlbf[rb],))
            vec(lambda e, y=yb: e.tensor_copy(out=rl_f[0:64, :], in_=YB[y][0:64, :]),
                reads=(b_YB[yb],), writes=(b_rlf,))

        def tail_b(h):
            chunk, prow, rb = h // 2, (h % 2) * 64, h % 2
            comp("tensor", lambda e, r=rb: e.matmul(
                RLB[:], ones_rl[64:65, 0:64], rl_bf[r][64:65, :], start=True, stop=True),
                reads=(b_rlbf[rb], b_consts), writes=(b_RLB,))
            ysl = Yf[prow:prow + 64, chunk * T + qc * 512: chunk * T + (qc + 1) * 512]
            vec(lambda e, o=ysl: e.tensor_tensor(
                out=o, in0=rl_f[0:64, :], in1=RLB[0:64, :], op=ALU.mult),
                reads=(b_rlf, b_RLB), writes=(b_Yf,))

        ybs = {}
        gate = attn_gate[0]
        done = 0
        for h in range(H):
            ybs[h], gate = head_stream(h, gate)
            if h >= 2:
                tail_b(h - 2)
            if h >= 1:
                tail_a(h - 1, ybs[h - 1])
            while done * H < (h + 1) * len(fills):
                fills[done]()
                done += 1
        tail_b(H - 2)
        tail_a(H - 1, ybs[H - 1])
        tail_b(H - 1)
        while done < len(fills):
            fills[done]()
            done += 1
        attn_gate[0] = gate

    def ln_stats_apply(u_slot, eps, out_slot, junk):
        vec(lambda e: e.tensor_add(out=stat[:, 10:11], in0=stat[:, 0:1], in1=stat[:, 1:2]),
            reads=(b_stat,), writes=(b_stat,))
        vec(lambda e: e.tensor_scalar(out=stat[:, 10:11], in0=stat[:, 10:11],
                                      scalar1=1.0 / C, scalar2=None, op0=ALU.mult),
            reads=(b_stat,), writes=(b_stat,))
        comp("scalar", lambda e, u=u_slot, j=junk: e.activation(
            out=F[j][:], in_=F[u][:], func=ACT.Square, accum_out=stat[:, 11:12]),
            reads=(b_F[u_slot],), writes=(b_F[junk], b_stat))
        vec(lambda e: e.tensor_scalar(out=stat[:, 13:14], in0=stat[:, 10:11],
                                      scalar1=stat[:, 10:11], scalar2=None, op0=ALU.mult),
            reads=(b_stat,), writes=(b_stat,))
        vec(lambda e: e.tensor_scalar(out=stat[:, 14:15], in0=stat[:, 11:12],
                                      scalar1=1.0 / C, scalar2=stat[:, 13:14],
                                      op0=ALU.mult, op1=ALU.subtract),
            reads=(b_stat,), writes=(b_stat,))
        comp("scalar", lambda e, ep=eps: e.activation(
            out=stat[:, 15:16], in_=stat[:, 14:15], func=ACT.Ln,
            bias=epsc[:, 0:1] if ep == LN1_EPS else epsc[:, 1:2]),
            reads=(b_stat, b_consts), writes=(b_stat,))
        comp("scalar", lambda e: e.activation(
            out=stat[:, 12:13], in_=stat[:, 15:16], func=ACT.Exp, scale=-0.5),
            reads=(b_stat,), writes=(b_stat,))
        vec(lambda e, u=u_slot, o=out_slot: e.tensor_scalar(
            out=F[o][:], in0=F[u][:], scalar1=stat[:, 10:11], scalar2=stat[:, 12:13],
            op0=ALU.subtract, op1=ALU.mult),
            reads=(b_F[u_slot], b_stat), writes=(b_F[out_slot],))

    def proj_ln_tb(s, tb):
        fslot = tb % 3
        dma("sync", F[fslot][:], x_h[s, tb * 128:(tb + 1) * 128, :], f"f{fslot}", wbuf=b_F[fslot])
        u = 3 + tb % 2
        for hf in range(2):
            pb = (tb * 2 + hf) % 4
            fns = [(lambda e, c=cc, h=hf, p=pb, t=tb: e.matmul(
                PR[p][:], Yf[:, c * T + t * 128: c * T + t * 128 + 128],
                Warena[:, c * C + h * 512: c * C + h * 512 + 512],
                start=(c == 0), stop=(c == 7))) for cc in range(8)]
            group("tensor", fns, reads=(b_Yf, b_Warena), writes=(b_PR[pb],))
            vec(lambda e, h=hf, p=pb, uu=u, fs=fslot: e.scalar_tensor_tensor(
                out=F[uu][:, h * 512:(h + 1) * 512], in0=PR[p][:], scalar=0.0,
                in1=F[fs][:, h * 512:(h + 1) * 512], op0=ALU.add, op1=ALU.add,
                accum_out=stat[:, h:h + 1]),
                reads=(b_PR[pb], b_F[fslot]), writes=(b_F[u], b_stat))
        x1s = 4 + tb % 2
        jslot = 6 + (1 - tb % 2)
        ln_stats_apply(u, LN1_EPS, x1s, jslot)
        k = seq_tb(s, tb)
        dma("gpsimd", x1_d[k * 128:(k + 1) * 128, :], F[x1s][:], "g1",
            rbuf=b_F[x1s], wbuf=b_x1d)
        for hf in range(2):
            vec(lambda e, h=hf, xs=x1s: e.scalar_tensor_tensor(
                out=F[7][:, h * 512:(h + 1) * 512], in0=F[xs][:, h * 512:(h + 1) * 512],
                scalar=0.0, in1=F[xs][:, h * 512:(h + 1) * 512],
                op0=ALU.add, op1=ALU.mult, accum_out=stat[:, 2 + h:3 + h]),
                reads=(b_F[x1s],), writes=(b_F[7], b_stat))
        vec(lambda e, xs=x1s: e.tensor_reduce(out=stat[:, 4:5], in_=F[xs][:],
                                              axis=AX, op=ALU.add),
            reads=(b_F[x1s],), writes=(b_stat,))
        vec(lambda e: e.tensor_scalar(out=stat[:, 4:5], in0=stat[:, 4:5],
                                      scalar1=1.0 / C, scalar2=None, op0=ALU.mult),
            reads=(b_stat,), writes=(b_stat,))
        vec(lambda e: e.tensor_add(out=stat[:, 2:3], in0=stat[:, 2:3], in1=stat[:, 3:4]),
            reads=(b_stat,), writes=(b_stat,))
        vec(lambda e: e.tensor_scalar(out=stat[:, 5:6], in0=stat[:, 4:5],
                                      scalar1=stat[:, 4:5], scalar2=None, op0=ALU.mult),
            reads=(b_stat,), writes=(b_stat,))
        vec(lambda e: e.tensor_scalar(out=stat[:, 6:7], in0=stat[:, 2:3],
                                      scalar1=1.0 / C, scalar2=stat[:, 5:6],
                                      op0=ALU.mult, op1=ALU.subtract),
            reads=(b_stat,), writes=(b_stat,))
        comp("scalar", lambda e: e.activation(out=stat[:, 7:8], in_=stat[:, 6:7],
                                              func=ACT.Ln, bias=epsc[:, 1:2]),
             reads=(b_stat, b_consts), writes=(b_stat,))
        comp("scalar", lambda e: e.activation(
            out=stat[:, 8:9], in_=stat[:, 7:8], func=ACT.Exp, scale=-0.5),
            reads=(b_stat,), writes=(b_stat,))
        vec(lambda e, xs=x1s: e.tensor_scalar(
            out=F[7][:], in0=F[xs][:], scalar1=stat[:, 4:5], scalar2=stat[:, 8:9],
            op0=ALU.subtract, op1=ALU.mult),
            reads=(b_F[x1s], b_stat), writes=(b_F[7],))
        vec(lambda e: e.scalar_tensor_tensor(out=F[7][:], in0=F[7][:], scalar=0.0,
                                             in1=Gb[:], op0=ALU.add, op1=ALU.mult),
            reads=(b_F[7], b_Gb), writes=(b_F[7],))
        vec(lambda e: e.tensor_add(out=F[7][:], in0=F[7][:], in1=Bb[:]),
            reads=(b_F[7], b_Bb), writes=(b_F[7],))
        vec(lambda e: e.tensor_reduce(out=gtmp[:, 0:1], in_=F[7][:], axis=AX,
                                      op=ALU.max, apply_absolute_value=True),
            reads=(b_F[7],), writes=(b_gtmp,))
        vec(lambda e: e.tensor_max(out=gacc[:], in0=gacc[:], in1=gtmp[:, 0:1]),
            reads=(b_gtmp, b_gacc), writes=(b_gacc,))
        dma("gpsimd", xn_d[k * 128:(k + 1) * 128, :], F[7][:], "g3",
            rbuf=b_F[7], wbuf=b_xnd)

    for s in range(NSEQ):
        if s > 0:
            dma("sync", Warena[:], wqv_d[:], "ar", rbuf=b_wqv, wbuf=b_Warena)
        qkv_tc(s, 0)
        for tc in range(NQC - 1):
            fl = qkv_units(s, tc + 1)
            if s == 0 and tc < 2:
                pu = [(lambda o=ob: prep_ob(o, deferred=True))
                      for ob in range(24 + tc * 8, 32 + tc * 8)]
                fl = [x for pair in zip(fl[:8], pu) for x in pair] + fl[8:]
                if tc == 1:
                    fl.append(lambda: vec(
                        lambda e: e.memset(vt3[:, 12:16, :, 64:65], 1.0),
                        writes=(b_Vt,)))
            attention_qc(s, tc, fl)
        dma("sync", Warena[:], wqp_d[:], "ar", rbuf=b_wqp, wbuf=b_Warena)
        attention_qc(s, NQC - 1,
                     [(lambda t=tb, ss=s: proj_ln_tb(ss, t)) for tb in range(12)])
        for tb in range(12, NTB):
            proj_ln_tb(s, tb)

    comp("tensor", lambda e: e.matmul(RLB[0:1, 0:128], gacc[:], ident_f[:],
                                      is_transpose=True, start=True, stop=True),
         reads=(b_gacc, b_consts), writes=(b_RLB,))
    vec(lambda e: e.tensor_reduce(out=sr(7), in_=RLB[0:1, 0:128], axis=AX, op=ALU.max),
        reads=(b_RLB,), writes=(b_scrow,))
    rec_done = vec(lambda e: e.tensor_copy(out=rec_row[:],
                                           in_=sr(7).broadcast_to([1, 8])),
                   reads=(b_scrow,), writes=(b_peer,))

    def gp_exchange(e):
        par = par_holder["par"]
        for p in (0, 1):
            with e.If_eq(par, p):
                e.dma_start(out=sh[p:p + 1, 0:8], in_=rec_row[:]).then_inc(psem, 16)
        e.wait_ge(psem, 16)
        for p in (0, 1):
            with e.If_eq(par, p):
                e.dma_start(out=sh_u[p:p + 1, 8:16], in_=nz_sb[:]).then_inc(psem, 16)
        e.wait_ge(psem, 32)
        r_tgt = e.alloc_register("tgt")
        r_val = e.alloc_register("val")
        r_non = e.alloc_register("non")
        r_it = e.alloc_register("it")
        e.reg_mov(r_tgt, 32)
        e.reg_mov(r_it, 0)
        e.reg_load(r_non, nz_sb[0:1, 0:1])

        def cond():
            e.reg_add(r_tgt, r_tgt, 16)
            e.reg_add(r_it, r_it, 1)
            for p in (0, 1):
                with e.If_eq(par, p):
                    e.dma_start(out=poll_row[:], in_=sh_u[1 - p:2 - p, 8:16]).then_inc(psem, 16)
            e.wait_ge(psem, r_tgt)
            e.reg_load(r_val, poll_row[0:1, 0:1])
            ne = e.alloc_register("ne")
            lt = e.alloc_register("lt")
            e.reg_alu(ne, r_val, r_non, ALU.not_equal)
            e.reg_alu(lt, r_it, 30000, ALU.is_lt)
            e.reg_alu(ne, ne, lt, ALU.logical_and)
            return ne

        with e.While(cond):
            pass
        for p in (0, 1):
            with e.If_eq(par, p):
                e.dma_start(out=peer_row[:], in_=sh[1 - p:2 - p, 0:8]).then_inc(psem, 16)
        e.reg_add(r_tgt, r_tgt, 16)
        e.wait_ge(psem, r_tgt)
        fl = e.alloc_register("fl")
        e.reg_load(r_val, poll_row[0:1, 0:1])
        e.reg_alu(fl, r_val, r_non, ALU.is_equal)
        e.store(poll_row.bitcast(I32)[0:1, 4:5], fl)
        return e.nop()

    if PAIR_GAMMA:
        ex_op = comp("gpsimd", gp_exchange, reads=(b_nz,), writes=(b_peer,),
                     extra=(rec_done,))
        vec(lambda e: e.tensor_copy(out=gtmp[0:1, 1:2],
                                    in_=poll_row.bitcast(I32)[0:1, 4:5]),
            reads=(b_peer,), writes=(b_gtmp,), extra=(ex_op,))
        vec(lambda e: e.tensor_tensor(out=gtmp[0:1, 2:3], in0=sr(7),
                                      in1=peer_row[0:1, 0:1], op=ALU.max),
            reads=(b_peer, b_scrow), writes=(b_gtmp,), extra=(ex_op,))
        vec(lambda e: e.tensor_sub(out=gtmp[0:1, 2:3], in0=gtmp[0:1, 2:3], in1=sr(7)),
            reads=(b_scrow, b_gtmp), writes=(b_gtmp,))
        vec(lambda e: e.scalar_tensor_tensor(out=gtmp[0:1, 2:3], in0=gtmp[0:1, 2:3],
                                             scalar=gtmp[0:1, 1:2], in1=sr(7),
                                             op0=ALU.mult, op1=ALU.add),
            reads=(b_gtmp, b_scrow), writes=(b_gtmp,))
        vec(lambda e: e.tensor_copy(out=sr(7), in_=gtmp[0:1, 2:3]),
            reads=(b_gtmp,), writes=(b_scrow,))
    vec(lambda e: e.reciprocal(out=sr(8), in_=sr(7)), reads=(b_scrow,), writes=(b_scrow,))
    vec(lambda e: e.tensor_scalar(out=sr(8), in0=sr(8), scalar1=128.0, scalar2=None,
                                  op0=ALU.mult),
        reads=(b_scrow,), writes=(b_scrow,))
    vec(lambda e: e.tensor_scalar(out=sr(9), in0=sr(7), scalar1=sr(6),
                                  scalar2=1.0 / 128.0, op0=ALU.mult, op1=ALU.mult),
        reads=(b_scrow,), writes=(b_scrow,))
    bcast_scrow()
    vec(lambda e: e.tensor_scalar(out=Bb[:], in0=Bb[:], scalar1=scl[:, 8:9],
                                  scalar2=None, op0=ALU.mult),
        reads=(b_Bb, b_scl), writes=(b_Bb,))
    dma("sync", Warena[:], wqf_d[:], "ar", rbuf=b_wqf, wbuf=b_Warena)

    def p2_front(k):
        xns, x1s = (k % 2) * 2, (k % 2) * 2 + 1
        xq = k % 2
        dma("sync", F[xns][:], xn_d[k * 128:(k + 1) * 128, :], f"f{xns}",
            rbuf=b_xnd, wbuf=b_F[xns])
        dma("scalar", F[x1s][:], x1_d[k * 128:(k + 1) * 128, :], f"f{x1s}",
            rbuf=b_x1d, wbuf=b_F[x1s])
        vec(lambda e, q=xq, xs=xns: e.tensor_scalar(
            out=xq_tm[q][:], in0=F[xs][:], scalar1=scl[:, 8:9], scalar2=CLIP,
            op0=ALU.mult, op1=ALU.min),
            reads=(b_F[xns], b_scl), writes=(b_Xbf[0],))
        vec(lambda e, q=xq: e.tensor_scalar(
            out=xq_tm[q][:], in0=xq_tm[q][:], scalar1=-CLIP, scalar2=None,
            op0=ALU.max),
            reads=(b_Xbf[0],), writes=(b_Xbf[0],))
        transpose8(xq_tm[xq], (b_Xbf[0],))
        vec(lambda e, q=xq: e.tensor_copy(out=xq_fm[q][:], in_=PSB[:]),
            reads=(b_PSB,), writes=(b_Xbf[1],))
        u = 4 + k % 2
        for hf in range(2):
            pb = (k * 2 + hf) % 4
            fns = [(lambda e, c=cc, h=hf, p=pb, q=xq: e.matmul(
                PR[p][:], xq_fm[q][:, c * 128:(c + 1) * 128],
                Warena[:, c * C + h * 512: c * C + h * 512 + 512],
                start=(c == 0), stop=(c == 7))) for cc in range(8)]
            group("tensor", fns, reads=(b_Xbf[1], b_Warena), writes=(b_PR[pb],))

    def p2_back(k):
        s, tb = divmod(k, NTB)
        x1s = (k % 2) * 2 + 1
        u = 4 + k % 2
        outs = 6 + k % 2
        for hf in range(2):
            pb = (k * 2 + hf) % 4
            vec(lambda e, h=hf, p=pb, uu=u, fs=x1s: e.scalar_tensor_tensor(
                out=F[uu][:, h * 512:(h + 1) * 512], in0=PR[p][:],
                scalar=scl[:, 9:10], in1=F[fs][:, h * 512:(h + 1) * 512],
                op0=ALU.mult, op1=ALU.add, accum_out=stat[:, h:h + 1]),
                reads=(b_PR[pb], b_F[x1s], b_scl), writes=(b_F[u], b_stat))
        ln_stats_apply(u, LN2_EPS, outs, outs)
        dma("gpsimd", out_h[s, tb * 128:(tb + 1) * 128, :], F[outs][:], "g2",
            rbuf=b_F[outs])

    NK = NSEQ * NTB
    p2_front(0)
    for i in range(NK):
        if i + 1 < NK:
            p2_front(i + 1)
        p2_back(i)

    S.finalize()
    finals = {}
    for op in S.ops["gpsimd"]:
        if op.val is not None and op.val[0] in ("g1", "g2", "g3", "sk", "sp", "sf", "sv"):
            finals[op.val[0]] = max(finals.get(op.val[0], 0), op.val[1])

    with nc.Block() as block:
        lp = lambda: nc.allow_low_precision(
            "bf16 matmul operands with fp32 accumulation (sanctioned); "
            "all normalisation math is fp32")

        @block.sync
        def _(e):
            with lp():
                S.replay("sync", e, sems)

        @block.scalar
        def _(e):
            with lp():
                S.replay("scalar", e, sems)

        @block.tensor
        def _(e):
            with lp():
                S.replay("tensor", e, sems)

        @block.vector
        def _(e):
            with lp():
                S.replay("vector", e, sems)

        @block.gpsimd
        def _(e):
            with lp():
                S.replay("gpsimd", e, sems)
            for n in sorted(finals):
                e.wait_ge(sems[n], finals[n])

    ctx.close()
    return nc


_cache = {}


def kernel(L0_attn_x, L0_attn_w_attn, L0_attn_w_proj, L0_ffn_weight,
           L0_ffn_ln_gamma, L0_ffn_ln_beta):
    x = np.ascontiguousarray(np.asarray(L0_attn_x, np.float32))
    wa = np.ascontiguousarray(np.asarray(L0_attn_w_attn, np.float32))
    wp = np.ascontiguousarray(np.asarray(L0_attn_w_proj, np.float32))
    wf = np.ascontiguousarray(np.asarray(L0_ffn_weight, np.float32))
    g = np.ascontiguousarray(np.asarray(L0_ffn_ln_gamma, np.float32)).reshape(1, C)
    b = np.ascontiguousarray(np.asarray(L0_ffn_ln_beta, np.float32)).reshape(1, C)
    nonce = np.full((1, 8), np.uint32((time.time_ns() >> 8) & 0x7FFFFFFF) | np.uint32(1),
                    dtype=np.uint32)
    if "nc" not in _cache:
        _cache["nc"] = build_nc()
    nc = _cache["nc"]
    shards = x.reshape(N_CORES, NSEQ, T, C)
    in_maps = [{"x": shards[i], "w_attn": wa, "w_proj": wp, "w_ffn": wf,
                "lng": g, "lnb": b, "nonce": nonce} for i in range(N_CORES)]
    res = run_bass_kernel_spmd(nc, in_maps, core_ids=list(range(N_CORES)))
    out = np.concatenate([r["out"].reshape(NSEQ, T, C) for r in res.results], axis=0)
    return out


# version 30 (final)
# speedup vs baseline: 1.0387x; 1.0077x over previous
import time
import numpy as np
import ml_dtypes
import concourse.bass as bass
import concourse.mybir as mybir
from concourse.bass_utils import run_bass_kernel_spmd

F32 = mybir.dt.float32
BF16 = mybir.dt.bfloat16
U32 = mybir.dt.uint32
I32 = mybir.dt.int32
ALU = mybir.AluOpType
ACT = mybir.ActivationFunctionType
AX = mybir.AxisListType.X

N_CORES = 8
NSEQ = 2
T = 2048
C = 1024
H = 16
NTB = T // 128
NQC = T // 512
CLIP = 128.0 - 1e-5
LN1_EPS = 1e-5
LN2_EPS = 1e-5
FFN_LN_EPS = 1e-3
PAIR_GAMMA = False

ENGS = ("sync", "scalar", "tensor", "vector", "gpsimd")
ESEM = {"sync": ("sy", 16), "scalar": ("sc", 1), "tensor": ("pe", 1),
        "vector": ("dv", 1), "gpsimd": ("gp", 1)}


class Op:
    __slots__ = ("eng", "fn", "waits", "sem", "needed", "val")

    def __init__(self, eng, fn, waits, sem):
        self.eng = eng
        self.fn = fn
        self.waits = [(w, r) for (w, r) in waits if w is not None]
        self.sem = sem
        self.needed = False
        self.val = None


class Sched:

    def __init__(self):
        self.ops = {e: [] for e in ENGS}
        self.order = []

    def emit(self, eng, fn, waits=(), dma=None):
        sem = (dma, 16) if dma else ESEM[eng]
        op = Op(eng, fn, waits, sem)
        if dma:
            op.needed = True
        self.ops[eng].append(op)
        self.order.append(op)
        return op

    @staticmethod
    def _edge_kept(op, w, raw):
        if w.eng != op.eng or w.sem[1] == 16:
            return True
        return raw and op.eng != "tensor"

    def finalize(self):
        for op in self.order:
            for w, raw in op.waits:
                if self._edge_kept(op, w, raw):
                    w.needed = True
        cnt = {}
        for op in self.order:
            if op.needed:
                name, inc = op.sem
                cnt[name] = cnt.get(name, 0) + inc
                op.val = (name, cnt[name])

    def replay(self, eng_name, eng, sems):
        waited = {}
        for op in self.ops[eng_name]:
            tgt = {}
            for w, raw in op.waits:
                if not self._edge_kept(op, w, raw):
                    continue
                if w.val is None:
                    continue
                n, v = w.val
                if v > tgt.get(n, -1):
                    tgt[n] = v
            for n, v in sorted(tgt.items()):
                if v > waited.get(n, 0):
                    eng.wait_ge(sems[n], v)
                    waited[n] = v
            inst = op.fn(eng)
            if op.needed and op.val is not None and inst is not None:
                inst.then_inc(sems[op.val[0]], op.sem[1])


class Buf:
    __slots__ = ("w", "r")

    def __init__(self):
        self.w = None
        self.r = []

    def rd_waits(self):
        return [self.w] if self.w is not None else []

    def wr_waits(self):
        return list(self.r) + ([self.w] if self.w is not None else [])

    def rd(self, op):
        self.r.append(op)

    def wrote(self, op):
        self.w = op
        self.r = []


def build_nc():
    nc = bass.Bass("TRN2", target_bir_lowering=False, detect_race_conditions=False)

    x_h = nc.dram_tensor("x", [NSEQ, T, C], F32, kind="ExternalInput").ap()
    wa_h = nc.dram_tensor("w_attn", [3 * C, C], F32, kind="ExternalInput").ap()
    wp_h = nc.dram_tensor("w_proj", [C, C], F32, kind="ExternalInput").ap()
    wf_h = nc.dram_tensor("w_ffn", [C, C], F32, kind="ExternalInput").ap()
    g_h = nc.dram_tensor("lng", [1, C], F32, kind="ExternalInput").ap()
    b_h = nc.dram_tensor("lnb", [1, C], F32, kind="ExternalInput").ap()
    nz_h = nc.dram_tensor("nonce", [1, 8], U32, kind="ExternalInput").ap()
    out_h = nc.dram_tensor("out", [NSEQ, T, C], F32, kind="ExternalOutput").ap()

    x1_d = nc.dram_tensor("x1_spill", [NSEQ * T, C], F32, kind="Internal").ap()
    xn_d = nc.dram_tensor("xn_spill", [NSEQ * T, C], F32, kind="Internal").ap()
    wqk_d = nc.dram_tensor("wq_qk", [128, 16 * C], BF16, kind="Internal").ap()
    wqv_d = nc.dram_tensor("wq_v", [128, 8 * C], BF16, kind="Internal").ap()
    wqp_d = nc.dram_tensor("wq_p", [128, 8 * C], BF16, kind="Internal").ap()
    wqf_d = nc.dram_tensor("wq_f", [128, 8 * C], BF16, kind="Internal").ap()
    sh = nc.dram_tensor("sh_pair", [2, 16], F32, kind="Internal", addr_space="Shared").ap()
    sh_u = sh.bitcast(U32)

    ident_bf_h = nc.inline_tensor(np.eye(128, dtype=ml_dtypes.bfloat16), "c_ident_bf").ap()
    ident_f_h = nc.inline_tensor(np.eye(128, dtype=np.float32), "c_ident_f").ap()
    mk = np.tril(np.ones((128, 128), np.float32)).T.astype(ml_dtypes.bfloat16)
    mask_h = nc.inline_tensor(mk, "causal_mask").ap()

    from contextlib import ExitStack
    ctx = ExitStack()
    sb = lambda n, s, d: ctx.enter_context(nc.sbuf_tensor(n, s, d))
    psumt = lambda n, s, d=F32: ctx.enter_context(nc.psum_tensor(n, s, d))

    budget = [0]

    def sbt(name, shape, dtype):
        per = int(np.prod(shape[1:])) * (2 if dtype == BF16 else 4)
        budget[0] += per
        return sb(name, shape, dtype)

    ident_bf = sbt("ident_bf", [128, 128], BF16)
    ident_f = sbt("ident_f", [128, 128], F32)
    mask_bf = sbt("mask_bf", [128, 128], BF16)
    ones_col = sbt("ones_col", [1, 128], F32)
    ones_128 = sbt("ones_128", [128, 1], F32)
    ones_rl = sbt("ones_rl", [128, 64], BF16)
    Gb = sbt("G_bcast", [128, C], F32)
    Bb = sbt("B_bcast", [128, C], F32)
    scl = sbt("scl", [128, 16], F32)
    sc_row = sbt("sc_row", [1, 16], F32)
    rec_row = sbt("rec_row", [1, 8], F32)
    peer_row = sbt("peer_row", [1, 8], F32)
    poll_row = sbt("poll_row", [1, 8], U32)
    nz_sb = sbt("nz_sb", [1, 8], U32)
    gacc = sbt("gacc", [128, 1], F32)
    gtmp = sbt("gtmp", [128, 4], F32)
    stat = sbt("stat", [128, 16], F32)
    wsum = sbt("wsum", [128, 80], F32)
    epsc = sbt("epsc", [128, 2], F32)
    Wqk_r = [sbt(f"Wqkr{i}", [128, C], BF16) for i in range(2)]
    Warena = sbt("Warena", [128, 8 * C], BF16)
    Kt = sbt("Kt", [128, 8 * T], BF16)
    Vt = sbt("Vt", [128, NTB * 1040], BF16)
    Yf = sbt("Yf", [128, 8 * T], BF16)
    Qr = [sbt(f"Qr{i}", [128, 8 * 512], BF16) for i in range(2)]
    Xbf = [sbt(f"Xbf{i}", [128, 8 * 512], BF16) for i in range(2)]
    Pt = [sbt(f"Pt{i}", [128, 512], BF16) for i in range(6)]
    xtmp = [sbt(f"xtmp{i}", [128, C], BF16) for i in range(2)]
    F = [sbt(f"F{i}", [128, C], F32) for i in range(8)]
    rl_f = sbt("rl_f", [65, 512], F32)
    rl_bf = [sbt(f"rl_bf{i}", [65, 512], BF16) for i in range(2)]

    assert budget[0] <= 211000, f"SBUF budget: {budget[0]} B/partition"

    xq_tm = [Xbf[0][:, 0:C], Xbf[0][:, C:2 * C]]
    xq_fm = [Xbf[1][:, 0:C], Xbf[1][:, C:2 * C]]
    wq_st = [Yf[:, i * C:(i + 1) * C] for i in range(3)]
    sg_st = [Yf[:, (3 + i) * C:(4 + i) * C] for i in range(3)]
    tr_st = [Yf[:, (6 + i) * C:(7 + i) * C] for i in range(2)]

    PB = [psumt(f"PB{i}", [128, 512]) for i in range(2)]
    YB = [psumt(f"YB{i}", [65, 512]) for i in range(2)]
    RLB = psumt("RLB", [64, 512])
    PSB = psumt("PSB", [128, 1024], BF16)
    SPB = [psumt(f"SPB{i}", [128, 512]) for i in range(2)]
    PR = [PB[0], PB[1], SPB[0], SPB[1]]

    sems = {n: nc.alloc_semaphore(name=n) for n in
            ("sc", "pe", "dv", "gp", "cn0", "cn1", "cn2", "cn3", "f0", "f1",
             "f2", "f3", "f4", "f5", "f6", "f7", "wk0", "wk1", "ar", "sk",
             "sp", "sf", "sv", "g1", "g2", "g3")}
    psem = nc.alloc_semaphore(name="poll")

    S = Sched()

    b_consts = Buf()
    b_Gb, b_Bb, b_scl, b_scrow = Buf(), Buf(), Buf(), Buf()
    b_wsum = Buf()
    b_wsum2 = Buf()
    b_Wqk_r = [Buf(), Buf()]
    b_Warena = Buf()
    b_Kt, b_Vt, b_Yf = Buf(), Buf(), Buf()
    b_Qr = [Buf(), Buf()]
    b_Xbf = [Buf(), Buf()]
    b_Pt = [Buf() for _ in range(6)]
    b_xtmp = [Buf(), Buf()]
    b_F = [Buf() for _ in range(8)]
    b_PB = [Buf(), Buf()]
    b_YB = [Buf(), Buf()]
    b_RLB, b_PSB = Buf(), Buf()
    b_SPB = [Buf(), Buf()]
    b_PR = [b_PB[0], b_PB[1], b_SPB[0], b_SPB[1]]
    b_rlf = Buf()
    b_rlbf = [Buf(), Buf()]
    b_gacc, b_gtmp, b_stat = Buf(), Buf(), Buf()
    b_x1d = Buf()
    b_xnd = Buf()
    b_wqk, b_wqv, b_wqp, b_wqf = Buf(), Buf(), Buf(), Buf()
    b_wqkL = [Buf() for _ in range(16)]
    b_nz, b_peer = Buf(), Buf()

    def dma(eng, out_ap, in_ap, sem, rbuf=None, wbuf=None, extra=()):
        waits = [(w, True) for w in extra]
        if rbuf is not None:
            waits += [(w, True) for w in rbuf.rd_waits()]
        if wbuf is not None:
            waits += [(w, False) for w in wbuf.wr_waits()]
        op = S.emit(eng, lambda e, o=out_ap, i=in_ap: e.dma_start(out=o, in_=i),
                    waits, dma=sem)
        if rbuf is not None:
            rbuf.rd(op)
        if wbuf is not None:
            wbuf.wrote(op)
        return op

    def comp(eng, fn, reads=(), writes=(), extra=()):
        waits = [(w, True) for w in extra]
        for b in reads:
            waits += [(w, True) for w in b.rd_waits()]
        for b in writes:
            waits += [(w, False) for w in b.wr_waits()]
        op = S.emit(eng, fn, waits)
        for b in reads:
            b.rd(op)
        for b in writes:
            b.wrote(op)
        return op

    def vec(fn, reads=(), writes=(), extra=()):
        return comp("vector", fn, reads, writes, extra)

    def group(eng, fns, reads=(), writes=(), extra=()):
        waits = [(w, True) for w in extra]
        for b in reads:
            waits += [(w, True) for w in b.rd_waits()]
        for b in writes:
            waits += [(w, False) for w in b.wr_waits()]
        ops = []
        for j, fn in enumerate(fns):
            ops.append(S.emit(eng, fn, waits if j == 0 else [(ops[-1], False)]))
        for b in reads:
            b.rd(ops[-1])
        for b in writes:
            b.wrote(ops[-1])
        return ops[-1]

    dma("sync", ident_bf[:], ident_bf_h[:], "cn0", wbuf=b_consts)
    dma("sync", ident_f[:], ident_f_h[:], "cn1", wbuf=b_consts)
    dma("sync", mask_bf[:], mask_h[:], "cn2", wbuf=b_consts)
    dma("sync", nz_sb[:], nz_h[:], "cn3", wbuf=b_nz)
    vec(lambda e: e.memset(ones_col[:], 1.0), writes=(b_consts,))
    vec(lambda e: e.memset(ones_128[:], 1.0), writes=(b_consts,))
    vec(lambda e: e.memset(ones_rl[:], 1.0), writes=(b_consts,))
    vec(lambda e: e.memset(gacc[:], 0.0), writes=(b_gacc,))
    vec(lambda e: e.memset(sc_row[:], 0.0), writes=(b_scrow,))
    vec(lambda e: e.memset(epsc[:, 0:1], LN1_EPS), writes=(b_consts,))
    vec(lambda e: e.memset(epsc[:, 1:2], FFN_LN_EPS), writes=(b_consts,))
    vt3 = Vt[:].rearrange("p (tb h f) -> p tb h f", tb=NTB, h=H, f=65)
    vec(lambda e: e.memset(vt3[:, :, :, 64:65], 1.0), writes=(b_Vt,))

    dma("sync", F[0][0:1, :], g_h[:], "f0", wbuf=b_F[0])
    dma("sync", F[1][0:1, :], b_h[:], "f1", wbuf=b_F[1])
    for src, bbuf, dst, bdst in ((F[0], b_F[0], Gb, b_Gb), (F[1], b_F[1], Bb, b_Bb)):
        for hf in range(2):
            comp("tensor", lambda e, s=src, h=hf: e.matmul(
                SPB[0][:, 0:512], ones_col[:], s[0:1, h * 512:(h + 1) * 512],
                start=True, stop=True),
                reads=(bbuf, b_consts), writes=(b_SPB[0],))
            vec(lambda e, h=hf, d=dst: e.tensor_copy(
                out=d[:, h * 512:(h + 1) * 512], in_=SPB[0][:, 0:512]),
                reads=(b_SPB[0],), writes=(bdst,))

    def wsrc(ob):
        if ob < 24:
            return wa_h[ob * 128:(ob + 1) * 128, :]
        if ob < 32:
            return wp_h[(ob - 24) * 128:(ob - 23) * 128, :]
        return wf_h[(ob - 32) * 128:(ob - 31) * 128, :]

    vec(lambda e: e.memset(wsum[:, 0:80], 0.0), writes=(b_wsum, b_wsum2))
    for ob in range(40):
        slot = ob % 3
        dma("sync" if ob % 2 == 0 else "scalar", F[slot][:], wsrc(ob),
            f"f{slot}", wbuf=b_F[slot])
        if ob % 2 == 0:
            vec(lambda e, s=slot, o=ob: e.tensor_reduce(
                out=wsum[:, o:o + 1], in_=F[s][:], axis=AX, op=ALU.add,
                apply_absolute_value=True),
                reads=(b_F[slot],), writes=(b_wsum,))
        else:
            comp("scalar", lambda e, s=slot, o=ob: e.activation(
                out=F[3][:], in_=F[s][:], func=ACT.Abs,
                accum_out=wsum[:, 40 + o:41 + o]),
                reads=(b_F[slot],), writes=(b_wsum2,))
    comp("tensor", lambda e: e.matmul(SPB[0][0:1, 0:40], ones_128[:], wsum[:, 0:40],
                                      start=True, stop=False),
         reads=(b_wsum, b_consts), writes=(b_SPB[0],))
    comp("tensor", lambda e: e.matmul(SPB[0][0:1, 0:40], ones_128[:], wsum[:, 40:80],
                                      start=False, stop=True),
         reads=(b_wsum2, b_consts), writes=(b_SPB[0],))

    def sr(c):
        return sc_row[0:1, c:c + 1]

    vec(lambda e: e.tensor_reduce(out=sr(2), in_=SPB[0][0:1, 0:24], axis=AX, op=ALU.add),
        reads=(b_SPB[0],), writes=(b_scrow,))
    vec(lambda e: e.tensor_reduce(out=sr(4), in_=SPB[0][0:1, 24:32], axis=AX, op=ALU.add),
        reads=(b_SPB[0],), writes=(b_scrow,))
    vec(lambda e: e.tensor_reduce(out=sr(6), in_=SPB[0][0:1, 32:40], axis=AX, op=ALU.add),
        reads=(b_SPB[0],), writes=(b_scrow,))
    n_attn, n_pf = float(24 * 128 * 1024), float(8 * 128 * 1024)
    vec(lambda e: e.tensor_scalar(out=sr(2), in0=sr(2), scalar1=1.0 / n_attn,
                                  scalar2=1e-8, op0=ALU.mult, op1=ALU.max),
        reads=(b_scrow,), writes=(b_scrow,))
    vec(lambda e: e.tensor_scalar(out=sr(4), in0=sr(4), scalar1=1.0 / n_pf,
                                  scalar2=1e-8, op0=ALU.mult, op1=ALU.max),
        reads=(b_scrow,), writes=(b_scrow,))
    vec(lambda e: e.tensor_scalar(out=sr(6), in0=sr(6), scalar1=1.0 / n_pf,
                                  scalar2=None, op0=ALU.mult),
        reads=(b_scrow,), writes=(b_scrow,))
    vec(lambda e: e.reciprocal(out=sr(0), in_=sr(2)), reads=(b_scrow,), writes=(b_scrow,))
    vec(lambda e: e.reciprocal(out=sr(3), in_=sr(4)), reads=(b_scrow,), writes=(b_scrow,))
    vec(lambda e: e.tensor_scalar(out=sr(15), in0=sr(6), scalar1=1e-5, scalar2=None,
                                  op0=ALU.add),
        reads=(b_scrow,), writes=(b_scrow,))
    vec(lambda e: e.reciprocal(out=sr(5), in_=sr(15)), reads=(b_scrow,), writes=(b_scrow,))
    vec(lambda e: e.tensor_scalar(out=sr(1), in0=sr(2), scalar1=sr(2), scalar2=0.125,
                                  op0=ALU.mult, op1=ALU.mult),
        reads=(b_scrow,), writes=(b_scrow,))
    comp("scalar", lambda e: e.activation(out=sr(10), in_=sr(4), func=ACT.Ln),
         reads=(b_scrow,), writes=(b_scrow,))

    def bcast_scrow():
        comp("tensor", lambda e: e.matmul(SPB[0][:, 0:16], ones_col[:], sc_row[:],
                                          start=True, stop=True),
             reads=(b_scrow, b_consts), writes=(b_SPB[0],))
        vec(lambda e: e.tensor_copy(out=scl[:], in_=SPB[0][:, 0:16]),
            reads=(b_SPB[0],), writes=(b_scl,))

    bcast_scrow()

    psb3 = PSB[:].rearrange("p (ch t) -> p ch t", ch=8, t=128)

    def transpose8(src_ap, rbufs):
        fns = [(lambda e, j=i, sp=src_ap: e.matmul(
            PSB[:, j * 128:(j + 1) * 128], sp[:, j * 128:(j + 1) * 128], ident_bf[:],
            is_transpose=True, start=(j == 0), stop=(j == 7))) for i in range(8)]
        return group("tensor", fns, reads=tuple(rbufs) + (b_consts,), writes=(b_PSB,))

    vtail = lambda i: Vt[:, 12480 + i * 1040:12480 + i * 1040 + C]

    def prep_ob(ob, deferred=False, queue=None):
        slot, qs = ob % 2, ob % 3
        dma(queue or ("sync" if ob % 2 == 0 else "scalar"), F[slot][:], wsrc(ob),
            f"f{slot}", wbuf=b_F[slot])
        sc_col = scl[:, 0:1] if ob < 24 else (scl[:, 3:4] if ob < 32 else scl[:, 5:6])
        a, q = 2 + qs, 5 + qs
        if deferred:
            wq_t, sg_t, tr_t, b_st = vtail(0), vtail(1), vtail(2), b_Vt
        else:
            wq_t, sg_t, tr_t, b_st = wq_st[qs], sg_st[qs], tr_st[ob % 2], b_Yf
        comp("scalar", lambda e, s=slot, aa=a, c=sc_col: e.activation(
            out=F[aa][:], in_=F[s][:], func=ACT.Abs, scale=c),
            reads=(b_F[slot], b_scl), writes=(b_F[a],))
        comp("scalar", lambda e, s=slot, o=sg_t: e.sign(out=o, in_=F[s][:]),
             reads=(b_F[slot],), writes=(b_st,))
        vec(lambda e, aa=a, qq=q: e.tensor_scalar(
            out=F[qq][:], in0=F[aa][:], scalar1=0.5, scalar2=None, op0=ALU.is_ge),
            reads=(b_F[a],), writes=(b_F[q],))
        if ob < 32:
            vec(lambda e, aa=a, qq=q: e.scalar_tensor_tensor(
                out=F[qq][:], in0=F[aa][:], scalar=1.5, in1=F[qq][:],
                op0=ALU.is_ge, op1=ALU.add),
                reads=(b_F[a], b_F[q]), writes=(b_F[q],))
        vec(lambda e, qq=q, wo=wq_t, so=sg_t: e.tensor_tensor(
            out=wo, in0=F[qq][:], in1=so, op=ALU.mult),
            reads=(b_F[q], b_st), writes=(b_st,))

        if ob >= 32:
            ch = ob - 32
            dma("gpsimd", wqf_d[:, ch * C:(ch + 1) * C], wq_t, "sf",
                rbuf=b_st, wbuf=b_wqf)
            return
        transpose8(wq_t, (b_st,))
        tr = tr_t
        if ob < 16:
            vec(lambda e, o=tr: e.tensor_copy(out=o, in_=psb3[:].rearrange("p a b -> p (a b)")),
                reads=(b_PSB,), writes=(b_st,))
            dma("gpsimd", wqk_d[:, ob * C:(ob + 1) * C], tr, "sk", rbuf=b_st,
                wbuf=b_wqkL[ob])
        elif ob < 24:
            wa3 = Warena[:].rearrange("p (ch v) -> p ch v", ch=8, v=C)
            vec(lambda e, o=ob - 16: e.tensor_copy(
                out=wa3[:, :, o * 128:(o + 1) * 128], in_=psb3[:]),
                reads=(b_PSB,), writes=(b_Warena,))
        else:
            vec(lambda e, o=tr: e.tensor_copy(out=o, in_=psb3[:].rearrange("p a b -> p (a b)")),
                reads=(b_PSB,), writes=(b_st,))
            o = ob - 24
            wq3d = wqp_d[:].rearrange("p (ch v) -> p ch v", ch=8, v=C)
            st3 = tr.rearrange("p (ch t) -> p ch t", ch=8, t=128)
            dma("gpsimd", wq3d[:, :, o * 128:(o + 1) * 128], st3[:], "sp", rbuf=b_st, wbuf=b_wqp)

    for ob in range(16, 24):
        prep_ob(ob)
    dma("gpsimd", wqv_d[:], Warena[:], "sv", rbuf=b_Warena, wbuf=b_wqv)

    par_holder = {}

    def gp_pre(e):
        par = e.alloc_register("par")
        for arm in e.switch_core_id(n=128):
            e.reg_mov(par, int(arm) % 2)
        par_holder["par"] = par
        return None

    S.emit("gpsimd", gp_pre, ())

    seq_tb = lambda s, tb: s * NTB + tb

    def qkv_units(s, tc):
        xs = tc % 2
        xb3 = Xbf[xs][:].rearrange("p (ch t) -> p ch t", ch=8, t=512)
        units = []

        def unit_tbl(tbl):
                tb = tc * 4 + tbl
                fslot, xt = tb % 3, tb % 2
                dma("sync", F[fslot][:], x_h[s, tb * 128:(tb + 1) * 128, :], f"f{fslot}", wbuf=b_F[fslot])
                comp("scalar", lambda e, fs=fslot, x=xt: e.activation(
                    out=xtmp[x][:], in_=F[fs][:], func=ACT.Copy),
                    reads=(b_F[fslot],), writes=(b_xtmp[xt],))
                transpose8(xtmp[xt], (b_xtmp[xt],))
                vec(lambda e, t=tbl: e.tensor_copy(
                    out=xb3[:, :, t * 128:(t + 1) * 128], in_=psb3[:]),
                    reads=(b_PSB,), writes=(b_Xbf[xs],))
                for hf in range(2):
                    pb = (tb * 2 + hf) % 4
                    fns = [(lambda e, c=cc, h=hf, t=tbl, p=pb: e.matmul(
                        PR[p][:], xb3[:, c, t * 128:(t + 1) * 128],
                        Warena[:, c * C + h * 512:c * C + h * 512 + 512],
                        start=(c == 0), stop=(c == 7))) for cc in range(8)]
                    group("tensor", fns, reads=(b_Xbf[xs], b_Warena), writes=(b_PR[pb],))
                    vo = vt3[:, tb, hf * 8:(hf + 1) * 8, 0:64]
                    vec(lambda e, o=vo, p=pb: e.tensor_scalar(
                        out=o, in0=PR[p][:], scalar1=scl[:, 2:3], scalar2=None, op0=ALU.mult),
                        reads=(b_PR[pb], b_scl), writes=(b_Vt,))

        def unit_ob(ob):
                wr = ob % 2
                dma("sync", Wqk_r[wr][:], wqk_d[:, ob * C:(ob + 1) * C], f"wk{wr}",
                    rbuf=b_wqkL[ob], wbuf=b_Wqk_r[wr])
                pb = ob % 4
                fns = [(lambda e, c=cc, w=wr, p=pb: e.matmul(
                    PR[p][:], Wqk_r[w][:, c * 128:(c + 1) * 128],
                    Xbf[xs][:, c * 512:(c + 1) * 512],
                    start=(c == 0), stop=(c == 7))) for cc in range(8)]
                group("tensor", fns, reads=(b_Wqk_r[wr], b_Xbf[xs]), writes=(b_PR[pb],))
                if ob < 8:
                    vec(lambda e, o=ob, p=pb: e.tensor_scalar(
                        out=Qr[tc % 2][:, o * 512:(o + 1) * 512], in0=PR[p][:],
                        scalar1=scl[:, 1:2], scalar2=None, op0=ALU.mult),
                        reads=(b_PR[pb], b_scl), writes=(b_Qr[tc % 2],))
                else:
                    vec(lambda e, o=ob - 8, p=pb: e.tensor_copy(
                        out=Kt[:, o * T + tc * 512: o * T + (tc + 1) * 512], in_=PR[p][:]),
                        reads=(b_PR[pb],), writes=(b_Kt,))

        for tbl in range(4):
            units.append(lambda t=tbl: unit_tbl(t))
        for ob in range(16):
            units.append(lambda o=ob: unit_ob(o))
        return units

    def qkv_tc(s, tc):
        for u in qkv_units(s, tc):
            u()

    attn_gate = [None]

    def attention_qc(s, qc, fills=()):
        qs = qc % 2
        nkb = 4 * qc + 4
        sring, bring = PR, b_PR

        def head_stream(h, exp_gate_in):
            yb = (qc * H + h) % 2
            chunk, prow = h // 2, (h % 2) * 64
            exp_ops = {}
            prod_ops = {}

            def geom(kb):
                return (kb - 4 * qc) * 128 if kb >= 4 * qc else 0

            def mm_S(kb, gate):
                qlo, r = geom(kb), kb % 4
                waits = ([(gate, True)] if gate is not None else [])
                waits += [(w, False) for w in bring[r].wr_waits()]
                op = S.emit("tensor", lambda e, k=kb, q=qlo, rr=r: e.matmul(
                    sring[rr][:, q:512],
                    Kt[prow:prow + 64, chunk * T + k * 128: chunk * T + (k + 1) * 128],
                    Qr[qs][prow:prow + 64, chunk * 512 + q:(chunk + 1) * 512],
                    start=True, stop=True),
                    waits)
                b_Kt.rd(op)
                b_Qr[qs].rd(op)
                bring[r].wrote(op)
                return op

            def do_exp(kb, gate):
                qlo, r, pr = geom(kb), kb % 4, kb % 6
                op = S.emit("scalar", lambda e, q=qlo, rr=r, p=pr: e.activation(
                    out=Pt[p][:, q:512], in_=sring[rr][:, q:512], func=ACT.Exp),
                    [(gate, True)] if gate is not None else [])
                op.needed = True
                bring[r].rd(op)
                b_Pt[pr].wrote(op)
                exp_ops[kb] = op
                if kb >= 4 * qc:
                    mop = S.emit("vector", lambda e, q=qlo, p=pr: e.tensor_tensor(
                        out=Pt[p][:, q:q + 128], in0=Pt[p][:, q:q + 128],
                        in1=mask_bf[:], op=ALU.mult), [(op, True)])
                    mop.needed = True
                    b_consts.rd(mop)
                    b_Pt[pr].wrote(mop)
                    prod_ops[kb] = mop
                else:
                    prod_ops[kb] = op
                return op

            def mm_AV(kb, gate):
                qlo, pr = geom(kb), kb % 6
                fn = lambda e, k=kb, q=qlo, p=pr: e.matmul(
                    YB[yb][:, q:512], Vt[:, k * 1040 + h * 65: k * 1040 + (h + 1) * 65],
                    Pt[p][:, q:512], start=(k == 0), stop=(k == nkb - 1))
                waits = [(gate, True)]
                if kb == 0:
                    waits += [(w, False) for w in b_YB[yb].wr_waits()]
                else:
                    waits += [(b_YB[yb].w, False)]
                op = S.emit("tensor", fn, waits)
                b_Vt.rd(op)
                b_Pt[pr].rd(op)
                b_YB[yb].wrote(op) if kb == nkb - 1 else b_YB[yb].__setattr__("w", op)

            for j in range(nkb // 2):
                a, b = 2 * j, 2 * j + 1
                gate = exp_ops.get(a - 3, exp_gate_in if a < 4 else None)
                mm_S(a, gate)
                sgb = mm_S(b, None)
                do_exp(a, sgb)
                do_exp(b, None)
                if a >= 4:
                    mm_AV(a - 4, prod_ops[a - 4])
                    mm_AV(b - 4, prod_ops[a - 3])
            for kk in range(max(0, nkb - 4), nkb):
                mm_AV(kk, prod_ops[kk])
            return yb, exp_ops[nkb - 1]

        def tail_a(h, yb):
            rb = h % 2
            comp("scalar", lambda e, y=yb: e.activation(
                out=rl_f[64:65, :], in_=YB[y][64:65, :], func=ACT.Ln),
                reads=(b_YB[yb],), writes=(b_rlf,))
            comp("scalar", lambda e, r=rb: e.activation(
                out=rl_bf[r][64:65, :], in_=rl_f[64:65, :], func=ACT.Exp,
                scale=-1.0, bias=scl[64:65, 10:11]),
                reads=(b_rlf, b_scl), writes=(b_rlbf[rb],))
            vec(lambda e, y=yb: e.tensor_copy(out=rl_f[0:64, :], in_=YB[y][0:64, :]),
                reads=(b_YB[yb],), writes=(b_rlf,))

        def tail_b(h):
            chunk, prow, rb = h // 2, (h % 2) * 64, h % 2
            comp("tensor", lambda e, r=rb: e.matmul(
                RLB[:], ones_rl[64:65, 0:64], rl_bf[r][64:65, :], start=True, stop=True),
                reads=(b_rlbf[rb], b_consts), writes=(b_RLB,))
            ysl = Yf[prow:prow + 64, chunk * T + qc * 512: chunk * T + (qc + 1) * 512]
            vec(lambda e, o=ysl: e.tensor_tensor(
                out=o, in0=rl_f[0:64, :], in1=RLB[0:64, :], op=ALU.mult),
                reads=(b_rlf, b_RLB), writes=(b_Yf,))

        ybs = {}
        gate = attn_gate[0]
        done = 0
        for h in range(H):
            ybs[h], gate = head_stream(h, gate)
            if h >= 2:
                tail_b(h - 2)
            if h >= 1:
                tail_a(h - 1, ybs[h - 1])
            while done * H < (h + 1) * len(fills):
                fills[done]()
                done += 1
        tail_b(H - 2)
        tail_a(H - 1, ybs[H - 1])
        tail_b(H - 1)
        while done < len(fills):
            fills[done]()
            done += 1
        attn_gate[0] = gate

    def ln_stats_apply(u_slot, eps, out_slot, junk):
        vec(lambda e: e.tensor_add(out=stat[:, 10:11], in0=stat[:, 0:1], in1=stat[:, 1:2]),
            reads=(b_stat,), writes=(b_stat,))
        vec(lambda e: e.tensor_scalar(out=stat[:, 10:11], in0=stat[:, 10:11],
                                      scalar1=1.0 / C, scalar2=None, op0=ALU.mult),
            reads=(b_stat,), writes=(b_stat,))
        comp("scalar", lambda e, u=u_slot, j=junk: e.activation(
            out=F[j][:], in_=F[u][:], func=ACT.Square, accum_out=stat[:, 11:12]),
            reads=(b_F[u_slot],), writes=(b_F[junk], b_stat))
        vec(lambda e: e.tensor_scalar(out=stat[:, 13:14], in0=stat[:, 10:11],
                                      scalar1=stat[:, 10:11], scalar2=None, op0=ALU.mult),
            reads=(b_stat,), writes=(b_stat,))
        vec(lambda e: e.tensor_scalar(out=stat[:, 14:15], in0=stat[:, 11:12],
                                      scalar1=1.0 / C, scalar2=stat[:, 13:14],
                                      op0=ALU.mult, op1=ALU.subtract),
            reads=(b_stat,), writes=(b_stat,))
        comp("scalar", lambda e, ep=eps: e.activation(
            out=stat[:, 15:16], in_=stat[:, 14:15], func=ACT.Ln,
            bias=epsc[:, 0:1] if ep == LN1_EPS else epsc[:, 1:2]),
            reads=(b_stat, b_consts), writes=(b_stat,))
        comp("scalar", lambda e: e.activation(
            out=stat[:, 12:13], in_=stat[:, 15:16], func=ACT.Exp, scale=-0.5),
            reads=(b_stat,), writes=(b_stat,))
        vec(lambda e, u=u_slot, o=out_slot: e.tensor_scalar(
            out=F[o][:], in0=F[u][:], scalar1=stat[:, 10:11], scalar2=stat[:, 12:13],
            op0=ALU.subtract, op1=ALU.mult),
            reads=(b_F[u_slot], b_stat), writes=(b_F[out_slot],))

    def proj_ln_tb(s, tb):
        fslot = tb % 3
        dma("sync", F[fslot][:], x_h[s, tb * 128:(tb + 1) * 128, :], f"f{fslot}", wbuf=b_F[fslot])
        u = 3 + tb % 2
        for hf in range(2):
            pb = (tb * 2 + hf) % 4
            fns = [(lambda e, c=cc, h=hf, p=pb, t=tb: e.matmul(
                PR[p][:], Yf[:, c * T + t * 128: c * T + t * 128 + 128],
                Warena[:, c * C + h * 512: c * C + h * 512 + 512],
                start=(c == 0), stop=(c == 7))) for cc in range(8)]
            group("tensor", fns, reads=(b_Yf, b_Warena), writes=(b_PR[pb],))
            vec(lambda e, h=hf, p=pb, uu=u, fs=fslot: e.scalar_tensor_tensor(
                out=F[uu][:, h * 512:(h + 1) * 512], in0=PR[p][:], scalar=0.0,
                in1=F[fs][:, h * 512:(h + 1) * 512], op0=ALU.add, op1=ALU.add,
                accum_out=stat[:, h:h + 1]),
                reads=(b_PR[pb], b_F[fslot]), writes=(b_F[u], b_stat))
        x1s = 4 + tb % 2
        jslot = 6 + (1 - tb % 2)
        ln_stats_apply(u, LN1_EPS, x1s, jslot)
        k = seq_tb(s, tb)
        dma("gpsimd", x1_d[k * 128:(k + 1) * 128, :], F[x1s][:], "g1",
            rbuf=b_F[x1s], wbuf=b_x1d)
        for hf in range(2):
            vec(lambda e, h=hf, xs=x1s: e.scalar_tensor_tensor(
                out=F[7][:, h * 512:(h + 1) * 512], in0=F[xs][:, h * 512:(h + 1) * 512],
                scalar=0.0, in1=F[xs][:, h * 512:(h + 1) * 512],
                op0=ALU.add, op1=ALU.mult, accum_out=stat[:, 2 + h:3 + h]),
                reads=(b_F[x1s],), writes=(b_F[7], b_stat))
        vec(lambda e, xs=x1s: e.tensor_reduce(out=stat[:, 4:5], in_=F[xs][:],
                                              axis=AX, op=ALU.add),
            reads=(b_F[x1s],), writes=(b_stat,))
        vec(lambda e: e.tensor_scalar(out=stat[:, 4:5], in0=stat[:, 4:5],
                                      scalar1=1.0 / C, scalar2=None, op0=ALU.mult),
            reads=(b_stat,), writes=(b_stat,))
        vec(lambda e: e.tensor_add(out=stat[:, 2:3], in0=stat[:, 2:3], in1=stat[:, 3:4]),
            reads=(b_stat,), writes=(b_stat,))
        vec(lambda e: e.tensor_scalar(out=stat[:, 5:6], in0=stat[:, 4:5],
                                      scalar1=stat[:, 4:5], scalar2=None, op0=ALU.mult),
            reads=(b_stat,), writes=(b_stat,))
        vec(lambda e: e.tensor_scalar(out=stat[:, 6:7], in0=stat[:, 2:3],
                                      scalar1=1.0 / C, scalar2=stat[:, 5:6],
                                      op0=ALU.mult, op1=ALU.subtract),
            reads=(b_stat,), writes=(b_stat,))
        comp("scalar", lambda e: e.activation(out=stat[:, 7:8], in_=stat[:, 6:7],
                                              func=ACT.Ln, bias=epsc[:, 1:2]),
             reads=(b_stat, b_consts), writes=(b_stat,))
        comp("scalar", lambda e: e.activation(
            out=stat[:, 8:9], in_=stat[:, 7:8], func=ACT.Exp, scale=-0.5),
            reads=(b_stat,), writes=(b_stat,))
        vec(lambda e, xs=x1s: e.tensor_scalar(
            out=F[7][:], in0=F[xs][:], scalar1=stat[:, 4:5], scalar2=stat[:, 8:9],
            op0=ALU.subtract, op1=ALU.mult),
            reads=(b_F[x1s], b_stat), writes=(b_F[7],))
        vec(lambda e: e.scalar_tensor_tensor(out=F[7][:], in0=F[7][:], scalar=0.0,
                                             in1=Gb[:], op0=ALU.add, op1=ALU.mult),
            reads=(b_F[7], b_Gb), writes=(b_F[7],))
        vec(lambda e: e.tensor_add(out=F[7][:], in0=F[7][:], in1=Bb[:]),
            reads=(b_F[7], b_Bb), writes=(b_F[7],))
        vec(lambda e: e.tensor_reduce(out=gtmp[:, 0:1], in_=F[7][:], axis=AX,
                                      op=ALU.max, apply_absolute_value=True),
            reads=(b_F[7],), writes=(b_gtmp,))
        vec(lambda e: e.tensor_max(out=gacc[:], in0=gacc[:], in1=gtmp[:, 0:1]),
            reads=(b_gtmp, b_gacc), writes=(b_gacc,))
        dma("gpsimd", xn_d[k * 128:(k + 1) * 128, :], F[7][:], "g3",
            rbuf=b_F[7], wbuf=b_xnd)

    for s in range(NSEQ):
        if s > 0:
            dma("sync", Warena[:], wqv_d[:], "ar", rbuf=b_wqv, wbuf=b_Warena)
        if s == 0:
            u0 = qkv_units(0, 0)
            for ob in range(4):
                prep_ob(ob, queue="scalar")
            for t in range(4):
                u0[t]()
            for k in range(16):
                if k + 4 < 16:
                    prep_ob(k + 4, queue="scalar")
                u0[4 + k]()
        else:
            qkv_tc(s, 0)
        for tc in range(NQC - 1):
            fl = qkv_units(s, tc + 1)
            if s == 0 and tc < 2:
                pu = [(lambda o=ob: prep_ob(o, deferred=True))
                      for ob in range(24 + tc * 8, 32 + tc * 8)]
                fl = [x for pair in zip(fl[:8], pu) for x in pair] + fl[8:]
                if tc == 1:
                    fl.append(lambda: vec(
                        lambda e: e.memset(vt3[:, 12:16, :, 64:65], 1.0),
                        writes=(b_Vt,)))
            attention_qc(s, tc, fl)
        dma("sync", Warena[:], wqp_d[:], "ar", rbuf=b_wqp, wbuf=b_Warena)
        attention_qc(s, NQC - 1,
                     [(lambda t=tb, ss=s: proj_ln_tb(ss, t)) for tb in range(12)])
        for tb in range(12, NTB):
            proj_ln_tb(s, tb)

    comp("tensor", lambda e: e.matmul(RLB[0:1, 0:128], gacc[:], ident_f[:],
                                      is_transpose=True, start=True, stop=True),
         reads=(b_gacc, b_consts), writes=(b_RLB,))
    vec(lambda e: e.tensor_reduce(out=sr(7), in_=RLB[0:1, 0:128], axis=AX, op=ALU.max),
        reads=(b_RLB,), writes=(b_scrow,))
    rec_done = vec(lambda e: e.tensor_copy(out=rec_row[:],
                                           in_=sr(7).broadcast_to([1, 8])),
                   reads=(b_scrow,), writes=(b_peer,))

    def gp_exchange(e):
        par = par_holder["par"]
        for p in (0, 1):
            with e.If_eq(par, p):
                e.dma_start(out=sh[p:p + 1, 0:8], in_=rec_row[:]).then_inc(psem, 16)
        e.wait_ge(psem, 16)
        for p in (0, 1):
            with e.If_eq(par, p):
                e.dma_start(out=sh_u[p:p + 1, 8:16], in_=nz_sb[:]).then_inc(psem, 16)
        e.wait_ge(psem, 32)
        r_tgt = e.alloc_register("tgt")
        r_val = e.alloc_register("val")
        r_non = e.alloc_register("non")
        r_it = e.alloc_register("it")
        e.reg_mov(r_tgt, 32)
        e.reg_mov(r_it, 0)
        e.reg_load(r_non, nz_sb[0:1, 0:1])

        def cond():
            e.reg_add(r_tgt, r_tgt, 16)
            e.reg_add(r_it, r_it, 1)
            for p in (0, 1):
                with e.If_eq(par, p):
                    e.dma_start(out=poll_row[:], in_=sh_u[1 - p:2 - p, 8:16]).then_inc(psem, 16)
            e.wait_ge(psem, r_tgt)
            e.reg_load(r_val, poll_row[0:1, 0:1])
            ne = e.alloc_register("ne")
            lt = e.alloc_register("lt")
            e.reg_alu(ne, r_val, r_non, ALU.not_equal)
            e.reg_alu(lt, r_it, 30000, ALU.is_lt)
            e.reg_alu(ne, ne, lt, ALU.logical_and)
            return ne

        with e.While(cond):
            pass
        for p in (0, 1):
            with e.If_eq(par, p):
                e.dma_start(out=peer_row[:], in_=sh[1 - p:2 - p, 0:8]).then_inc(psem, 16)
        e.reg_add(r_tgt, r_tgt, 16)
        e.wait_ge(psem, r_tgt)
        fl = e.alloc_register("fl")
        e.reg_load(r_val, poll_row[0:1, 0:1])
        e.reg_alu(fl, r_val, r_non, ALU.is_equal)
        e.store(poll_row.bitcast(I32)[0:1, 4:5], fl)
        return e.nop()

    if PAIR_GAMMA:
        ex_op = comp("gpsimd", gp_exchange, reads=(b_nz,), writes=(b_peer,),
                     extra=(rec_done,))
        vec(lambda e: e.tensor_copy(out=gtmp[0:1, 1:2],
                                    in_=poll_row.bitcast(I32)[0:1, 4:5]),
            reads=(b_peer,), writes=(b_gtmp,), extra=(ex_op,))
        vec(lambda e: e.tensor_tensor(out=gtmp[0:1, 2:3], in0=sr(7),
                                      in1=peer_row[0:1, 0:1], op=ALU.max),
            reads=(b_peer, b_scrow), writes=(b_gtmp,), extra=(ex_op,))
        vec(lambda e: e.tensor_sub(out=gtmp[0:1, 2:3], in0=gtmp[0:1, 2:3], in1=sr(7)),
            reads=(b_scrow, b_gtmp), writes=(b_gtmp,))
        vec(lambda e: e.scalar_tensor_tensor(out=gtmp[0:1, 2:3], in0=gtmp[0:1, 2:3],
                                             scalar=gtmp[0:1, 1:2], in1=sr(7),
                                             op0=ALU.mult, op1=ALU.add),
            reads=(b_gtmp, b_scrow), writes=(b_gtmp,))
        vec(lambda e: e.tensor_copy(out=sr(7), in_=gtmp[0:1, 2:3]),
            reads=(b_gtmp,), writes=(b_scrow,))
    vec(lambda e: e.reciprocal(out=sr(8), in_=sr(7)), reads=(b_scrow,), writes=(b_scrow,))
    vec(lambda e: e.tensor_scalar(out=sr(8), in0=sr(8), scalar1=128.0, scalar2=None,
                                  op0=ALU.mult),
        reads=(b_scrow,), writes=(b_scrow,))
    vec(lambda e: e.tensor_scalar(out=sr(9), in0=sr(7), scalar1=sr(6),
                                  scalar2=1.0 / 128.0, op0=ALU.mult, op1=ALU.mult),
        reads=(b_scrow,), writes=(b_scrow,))
    bcast_scrow()
    vec(lambda e: e.tensor_scalar(out=Bb[:], in0=Bb[:], scalar1=scl[:, 8:9],
                                  scalar2=None, op0=ALU.mult),
        reads=(b_Bb, b_scl), writes=(b_Bb,))
    dma("sync", Warena[:], wqf_d[:], "ar", rbuf=b_wqf, wbuf=b_Warena)

    def p2_front(k):
        xns, x1s = (k % 2) * 2, (k % 2) * 2 + 1
        xq = k % 2
        dma("sync", F[xns][:], xn_d[k * 128:(k + 1) * 128, :], f"f{xns}",
            rbuf=b_xnd, wbuf=b_F[xns])
        dma("scalar", F[x1s][:], x1_d[k * 128:(k + 1) * 128, :], f"f{x1s}",
            rbuf=b_x1d, wbuf=b_F[x1s])
        vec(lambda e, q=xq, xs=xns: e.tensor_scalar(
            out=xq_tm[q][:], in0=F[xs][:], scalar1=scl[:, 8:9], scalar2=CLIP,
            op0=ALU.mult, op1=ALU.min),
            reads=(b_F[xns], b_scl), writes=(b_Xbf[0],))
        vec(lambda e, q=xq: e.tensor_scalar(
            out=xq_tm[q][:], in0=xq_tm[q][:], scalar1=-CLIP, scalar2=None,
            op0=ALU.max),
            reads=(b_Xbf[0],), writes=(b_Xbf[0],))
        transpose8(xq_tm[xq], (b_Xbf[0],))
        vec(lambda e, q=xq: e.tensor_copy(out=xq_fm[q][:], in_=PSB[:]),
            reads=(b_PSB,), writes=(b_Xbf[1],))
        u = 4 + k % 2
        for hf in range(2):
            pb = (k * 2 + hf) % 4
            fns = [(lambda e, c=cc, h=hf, p=pb, q=xq: e.matmul(
                PR[p][:], xq_fm[q][:, c * 128:(c + 1) * 128],
                Warena[:, c * C + h * 512: c * C + h * 512 + 512],
                start=(c == 0), stop=(c == 7))) for cc in range(8)]
            group("tensor", fns, reads=(b_Xbf[1], b_Warena), writes=(b_PR[pb],))

    def p2_back(k):
        s, tb = divmod(k, NTB)
        x1s = (k % 2) * 2 + 1
        u = 4 + k % 2
        outs = 6 + k % 2
        for hf in range(2):
            pb = (k * 2 + hf) % 4
            vec(lambda e, h=hf, p=pb, uu=u, fs=x1s: e.scalar_tensor_tensor(
                out=F[uu][:, h * 512:(h + 1) * 512], in0=PR[p][:],
                scalar=scl[:, 9:10], in1=F[fs][:, h * 512:(h + 1) * 512],
                op0=ALU.mult, op1=ALU.add, accum_out=stat[:, h:h + 1]),
                reads=(b_PR[pb], b_F[x1s], b_scl), writes=(b_F[u], b_stat))
        ln_stats_apply(u, LN2_EPS, outs, outs)
        dma("gpsimd", out_h[s, tb * 128:(tb + 1) * 128, :], F[outs][:], "g2",
            rbuf=b_F[outs])

    NK = NSEQ * NTB
    p2_front(0)
    for i in range(NK):
        if i + 1 < NK:
            p2_front(i + 1)
        p2_back(i)

    S.finalize()
    finals = {}
    for op in S.ops["gpsimd"]:
        if op.val is not None and op.val[0] in ("g1", "g2", "g3", "sk", "sp", "sf", "sv"):
            finals[op.val[0]] = max(finals.get(op.val[0], 0), op.val[1])

    with nc.Block() as block:
        lp = lambda: nc.allow_low_precision(
            "bf16 matmul operands with fp32 accumulation (sanctioned); "
            "all normalisation math is fp32")

        @block.sync
        def _(e):
            with lp():
                S.replay("sync", e, sems)

        @block.scalar
        def _(e):
            with lp():
                S.replay("scalar", e, sems)

        @block.tensor
        def _(e):
            with lp():
                S.replay("tensor", e, sems)

        @block.vector
        def _(e):
            with lp():
                S.replay("vector", e, sems)

        @block.gpsimd
        def _(e):
            with lp():
                S.replay("gpsimd", e, sems)
            for n in sorted(finals):
                e.wait_ge(sems[n], finals[n])

    ctx.close()
    return nc


_cache = {}


def kernel(L0_attn_x, L0_attn_w_attn, L0_attn_w_proj, L0_ffn_weight,
           L0_ffn_ln_gamma, L0_ffn_ln_beta):
    x = np.ascontiguousarray(np.asarray(L0_attn_x, np.float32))
    wa = np.ascontiguousarray(np.asarray(L0_attn_w_attn, np.float32))
    wp = np.ascontiguousarray(np.asarray(L0_attn_w_proj, np.float32))
    wf = np.ascontiguousarray(np.asarray(L0_ffn_weight, np.float32))
    g = np.ascontiguousarray(np.asarray(L0_ffn_ln_gamma, np.float32)).reshape(1, C)
    b = np.ascontiguousarray(np.asarray(L0_ffn_ln_beta, np.float32)).reshape(1, C)
    nonce = np.full((1, 8), np.uint32((time.time_ns() >> 8) & 0x7FFFFFFF) | np.uint32(1),
                    dtype=np.uint32)
    if "nc" not in _cache:
        _cache["nc"] = build_nc()
    nc = _cache["nc"]
    shards = x.reshape(N_CORES, NSEQ, T, C)
    in_maps = [{"x": shards[i], "w_attn": wa, "w_proj": wp, "w_ffn": wf,
                "lng": g, "lnb": b, "nonce": nonce} for i in range(N_CORES)]
    res = run_bass_kernel_spmd(nc, in_maps, core_ids=list(range(N_CORES)))
    out = np.concatenate([r["out"].reshape(NSEQ, T, C) for r in res.results], axis=0)
    return out
